# version 06
# baseline (speedup 1.0000x reference)
import numpy as np
import concourse.bass as bass
import concourse.mybir as mybir
from concourse.bass_utils import run_bass_kernel_spmd

F32 = mybir.dt.float32
BF16 = mybir.dt.bfloat16
I32 = mybir.dt.int32
AF = mybir.ActivationFunctionType
ALU = mybir.AluOpType
AX = mybir.AxisListType

B, S, D = 16, 2048, 1024
NCORES = 8
SEQ_PER_CORE = B // NCORES
NT = S // 128
NC = D // 128
H = 16
HD = 64
NPAIR = 8
I0 = 2816
I1 = 4096
M1 = 256
EPS_LN = 1e-5
MAGIC = 12582912.0
MAGIC_I = 0x4B400000

WEIGHT_NAMES = [
    "L0_attn_q_w", "L0_attn_k_w", "L0_attn_v_w", "L0_attn_o_w",
    "L0_ffn_w_gate", "L0_ffn_w_up", "L0_ffn_w_down",
    "L1_attn_Wq", "L1_attn_bq", "L1_attn_Wk", "L1_attn_bk",
    "L1_attn_Wv", "L1_attn_bv", "L1_attn_Wo", "L1_attn_bo",
    "L1_attn_random_matrix",
    "L1_ffn_w_gate", "L1_ffn_w_up", "L1_ffn_w_down",
]
SLOPES = [2.0 ** (-0.5 * (h + 1)) for h in range(H)]


def po_off(it):
    return (it // 7) * 512 + (it % 7) * 65


class Buf:
    __slots__ = ("name", "last_w", "readers")

    def __init__(self, name):
        self.name = name
        self.last_w = None
        self.readers = []


class Prog:
    COMPUTE = ("pe", "dve", "act", "pool")
    QUEUES = ("qs",)

    def __init__(self):
        self.stream = {e: [] for e in self.COMPUTE + self.QUEUES}
        self.order = []

    def _deps(self, reads, writes):
        deps = set()
        for b in reads:
            if b.last_w is not None:
                deps.add(b.last_w)
        for b in writes:
            if b.last_w is not None:
                deps.add(b.last_w)
            deps.update(b.readers)
        return deps

    def _commit(self, eng, rec, reads, writes):
        tok = (eng, len(self.stream[eng]))
        rec["tok"] = tok
        self.stream[eng].append(rec)
        self.order.append(tok)
        for b in reads:
            b.readers.append(tok)
        for b in writes:
            b.last_w = tok
            b.readers = []
        return tok

    def op(self, eng, emit, reads=(), writes=(), tag="", drain=False):
        assert eng in self.COMPUTE
        deps = self._deps(reads, writes)
        return self._commit(eng, dict(emit=emit, deps=deps, dma=False, tag=tag,
                                      drain=drain), reads, writes)

    def dma(self, emit, reads=(), writes=(), tag=""):
        deps = self._deps(reads, writes)
        return self._commit("qs", dict(emit=emit, deps=deps, dma=True, tag=tag),
                            reads, writes)

    def finalize(self, nc, final_tokens):
        known = {e: {} for e in self.stream}
        own_sync = {e: -1 for e in self.stream}
        ksnap = {}
        inc_needed = {e: set() for e in self.stream}
        for tok in self.order:
            eng, idx = tok
            rec = self.stream[eng][idx]
            k = known[eng]
            waits = []
            for (fe, fj) in sorted(rec["deps"]):
                if fe == eng and fe not in self.QUEUES:
                    if eng == "pe":
                        continue
                    if idx - fj > 4 or own_sync[eng] >= fj:
                        continue
                    own_sync[eng] = fj
                    waits.append((fe, fj))
                    inc_needed[fe].add(fj)
                    continue
                if k.get(fe, -1) >= fj:
                    continue
                waits.append((fe, fj))
                inc_needed[fe].add(fj)
                for ge, gj in ksnap[(fe, fj)].items():
                    if k.get(ge, -1) < gj:
                        k[ge] = gj
                k[fe] = max(k.get(fe, -1), fj)
            rec["waits"] = waits
            if eng not in self.QUEUES:
                k[eng] = idx
                ksnap[tok] = dict(k)
            else:
                snap = dict(k)
                snap[eng] = idx
                ksnap[tok] = snap
        for (fe, fj) in final_tokens:
            inc_needed[fe].add(fj)
        semval = {}
        for e, ops in self.stream.items():
            v = 0
            for j, rec in enumerate(ops):
                rec["inc"] = (e in self.QUEUES) or (j in inc_needed[e])
                if rec["inc"]:
                    v += 16 if e in self.QUEUES else 1
                rec["semval"] = v
            semval[e] = v
        sems = {e: nc.alloc_semaphore("tl_" + e) for e in self.stream}

        def val_of(fe, fj):
            return self.stream[fe][fj]["semval"]

        def body(engobj, e):
            for rec in self.stream[e]:
                for (fe, fj) in rec["waits"]:
                    engobj.wait_ge(sems[fe], val_of(fe, fj))
                inst = rec["emit"](engobj)
                if rec["inc"]:
                    inst.then_inc(sems[e], 16 if rec["dma"] else 1)

        with nc.Block() as block:
            @block.tensor
            def _(t):
                body(t, "pe")

            @block.vector
            def _(v):
                body(v, "dve")

            @block.scalar
            def _(s):
                body(s, "act")

            @block.gpsimd
            def _(g):
                body(g, "pool")

            @block.sync
            def _(sp):
                body(sp, "qs")
                for (fe, fj) in final_tokens:
                    sp.wait_ge(sems[fe], val_of(fe, fj))


class Ctx:
    pass


SB_BASE = 16512


def _sb_at(c, name, shape, dtype, off):
    off = off + SB_BASE
    t = c.nc.alloc_sbuf_tensor_at(name, shape, dtype, offset=off)
    c.sb[name] = t
    itemsize = {F32: 4, BF16: 2, I32: 4}[dtype]
    per_part = int(np.prod(shape[1:])) * itemsize
    shared = None
    for o, (lo, hi) in c.ranges.items():
        if lo == off and hi == off + per_part:
            shared = c.sbuf_buf[o]
            break
    c.sbuf_buf[name] = shared if shared is not None else Buf(name)
    c.ranges[name] = (off, off + per_part)
    return t


def claim(c, *names):
    for n in names:
        lo, hi = c.ranges[n]
        me = c.sbuf_buf[n]
        for o, (lo2, hi2) in c.ranges.items():
            ob = c.sbuf_buf[o]
            if ob is me or hi2 <= lo or hi <= lo2:
                continue
            if ob.last_w is not None:
                me.readers.append(ob.last_w)
            me.readers.extend(ob.readers)
            ob.last_w = None
            ob.readers = []


def pob(c, c0, c1):
    return [c.ps_buf["PO%d" % b] for b in range(c0 // 512, (c1 + 511) // 512)]


def sbuf(c, n):
    return c.sb[n]


def bf(c, n):
    return c.sbuf_buf[n] if n in c.sbuf_buf else c.ps_buf[n]


DEBUG_DUMPS = []
DEBUG_SNAP = False
_SNAP_N = [0]


def dbg_snap(c, label, name, ap=None):
    if not DEBUG_SNAP:
        return
    t = c.sb[name]
    a = ap if ap is not None else t[:]
    dout = c.nc.dram_tensor("snap%d_%s" % (_SNAP_N[0], label), list(a.shape),
                            a.dtype, kind="ExternalOutput")
    _SNAP_N[0] += 1
    tok = c.p.dma(lambda e: e.dma_start(out=dout.ap(), in_=a),
                  reads=[bf(c, name)], writes=[], tag="snap")
    c.snap_toks.append(tok)


RACE_DETECT = True
P2_TRUNC = 0


def build_program(phases=(0, 1, 2, 3)):
    nc = bass.Bass("TRN2", target_bir_lowering=False,
                   detect_race_conditions=RACE_DETECT)
    c = Ctx()
    c.nc = nc
    c.p = Prog()
    c.sb, c.sbuf_buf, c.ranges = {}, {}, {}
    c.snap_toks = []
    c.ps, c.ps_buf = {}, {}
    c.phases = tuple(phases)

    shapes = {
        "L0_attn_q_w": [D, D], "L0_attn_k_w": [D, D], "L0_attn_v_w": [D, D],
        "L0_attn_o_w": [D, D],
        "L0_ffn_w_gate": [I0, D], "L0_ffn_w_up": [I0, D], "L0_ffn_w_down": [D, I0],
        "L1_attn_Wq": [D, D], "L1_attn_bq": [D], "L1_attn_Wk": [D, D],
        "L1_attn_bk": [D], "L1_attn_Wv": [D, D], "L1_attn_bv": [D],
        "L1_attn_Wo": [D, D], "L1_attn_bo": [D],
        "L1_attn_random_matrix": [HD, M1],
        "L1_ffn_w_gate": [I1, D], "L1_ffn_w_up": [I1, D], "L1_ffn_w_down": [D, I1],
    }
    c.x_in = nc.dram_tensor("x", [SEQ_PER_CORE, S, D], F32, kind="ExternalInput")
    c.w_in = {n: nc.dram_tensor(n, shp, F32, kind="ExternalInput")
              for n, shp in shapes.items()}
    c.out = nc.dram_tensor("out", [SEQ_PER_CORE, S, D], F32, kind="ExternalOutput")
    c.ws = {}
    for n, shp in [("qb", [D, D]), ("kb", [D, D]), ("vb", [D, D]), ("ob", [D, D]),
                   ("Wqb", [D, D]), ("Wkb", [D, D]), ("Wvb", [D, D]), ("Wob", [D, D]),
                   ("RMb", [HD, M1]),
                   ("wgT0", [D, I0]), ("wuT0", [D, I0]), ("wdT0", [I0, D]),
                   ("wgT1", [D, I1]), ("wuT1", [D, I1]), ("wdT1", [I1, D])]:
        c.ws[n] = nc.dram_tensor("ws_" + n, shp, BF16, kind="Internal")
    c.xs = nc.dram_tensor("xs_scratch", [S, D], F32, kind="Internal")
    c.hb = {n: Buf("hbm_" + n) for n in list(c.ws) + ["xs", "out", "x"]}

    _sb_at(c, "X", [128, NT, D], F32, 0)
    _sb_at(c, "WG0", [128, NC, I0], BF16, 0)
    _sb_at(c, "WDb0", [128, 6, D], BF16, 45056)
    _sb_at(c, "H1", [128, 8, I1], BF16, 0)
    _sb_at(c, "XT", [128, NC, S], BF16, 65536)
    _sb_at(c, "BB", [128, NT, D], BF16, 98304)
    _sb_at(c, "WDa0", [128, 16, D], BF16, 98304)
    _sb_at(c, "WD1a", [128, 16, D], BF16, 98304)
    _sb_at(c, "OW", [128, NC, D], BF16, 131072)
    _sb_at(c, "WU0", [128, NC, I0], BF16, 131072)
    _sb_at(c, "WG1c", [128, NC, 1024], BF16, 131072)
    _sb_at(c, "WU1c", [128, NC, 1024], BF16, 147456)
    _sb_at(c, "WD1b", [128, 16, D], BF16, 131072)
    _sb_at(c, "QP", [128, S], BF16, 147456)
    _sb_at(c, "KP", [128, S], BF16, 151552)
    _sb_at(c, "VP", [128, 2, NT, HD + 1], BF16, 155648)
    _sb_at(c, "WS1", [128, NC, 128], BF16, 159872)
    _sb_at(c, "WS2", [128, NC, 128], BF16, 161920)
    _sb_at(c, "QB1", [128, 2, S], BF16, 163968)
    _sb_at(c, "PT2", [128, S], BF16, 163968)
    _sb_at(c, "WB2", [128, 1024], F32, 168064)
    _sb_at(c, "QOUT2", [128, 1024], BF16, 172160)
    _sb_at(c, "HQT2", [128, 8, 128], BF16, 174208)
    _sb_at(c, "KB1", [128, NT, M1], BF16, 172160)
    E = 180352
    off = [E]

    def nxt(n):
        o = off[0]
        off[0] = (o + n + 31) // 32 * 32
        return o

    wb_off = nxt(4096)
    _sb_at(c, "WB", [128, 1024], F32, wb_off)
    _sb_at(c, "ROWSC", [1, S], BF16, wb_off)
    sc_off = nxt(4096)
    _sb_at(c, "SC", [128, 1024], F32, sc_off)
    _sb_at(c, "QH", [67, S], BF16, sc_off)
    sc2_off = nxt(4096)
    _sb_at(c, "SC2", [128, 1024], F32, sc2_off)
    _sb_at(c, "KH", [67, S], BF16, sc2_off)
    qi_off = nxt(4096)
    _sb_at(c, "QI", [128, 1024], I32, qi_off)
    _sb_at(c, "PT", [128, S], BF16, qi_off)
    qi2_off = nxt(4096)
    _sb_at(c, "QI2", [128, 1024], I32, qi2_off)
    _sb_at(c, "NIO", [65, S], BF16, qi2_off)
    _sb_at(c, "QOUT", [128, 1024], BF16, nxt(2048))
    _sb_at(c, "HQT", [128, 8, 128], BF16, nxt(2048))
    _sb_at(c, "ST", [128, 64], F32, nxt(256))
    _sb_at(c, "ST2", [128, 64], F32, nxt(256))
    _sb_at(c, "CN", [128, 8], F32, nxt(32))
    _sb_at(c, "IOTA", [128, 16], F32, nxt(64))
    _sb_at(c, "BIAS", [128, 16], F32, nxt(64))
    _sb_at(c, "ONES2", [128, 2], BF16, nxt(4))
    _sb_at(c, "TRI", [128, 128], BF16, nxt(256))
    _sb_at(c, "NEGI", [128, 128], BF16, nxt(256))
    _sb_at(c, "IDN", [128, 128], BF16, nxt(256))
    _sb_at(c, "NRM", [2, 16], F32, nxt(64))
    _sb_at(c, "MB", [128, 1], F32, nxt(4))
    _sb_at(c, "ONE1", [1, 128], BF16, nxt(256))
    _sb_at(c, "ONEC", [128, 1], BF16, nxt(2))
    _sb_at(c, "BCAST", [1, 8], BF16, nxt(16))
    _sb_at(c, "MCOL", [2, 4], BF16, nxt(8))
    _sb_at(c, "BCOL", [128, 4], F32, nxt(16))
    _sb_at(c, "ROWB", [1, 1024], BF16, nxt(2048))
    _sb_at(c, "KVS", [128, 2, HD + 1], BF16, nxt(260))
    _sb_at(c, "RM", [128, M1], BF16, nxt(512))
    _sb_at(c, "SM", [128, 64], F32, nxt(256))
    _sb_at(c, "SM2", [128, 64], F32, nxt(256))
    _sb_at(c, "BV8", [128, 8], F32, nxt(32))
    _sb_at(c, "BV8B", [128, 8], BF16, nxt(16))
    assert off[0] + SB_BASE < 229344, off[0]

    for name, shape, dt in [("PSS", [128, 2048], F32), ("PO", [128, 1536], F32),
                            ("PX", [128, 1024], BF16)]:
        c.ps[name] = nc.alloc_psum_tensor(name, shape, dt)
        c.ps_buf[name] = Buf("ps_" + name)
    for q in range(4):
        c.ps_buf["PSS%d" % q] = Buf("ps_PSS%d" % q)
    for q in range(3):
        c.ps_buf["PO%d" % q] = Buf("ps_PO%d" % q)

    emit_consts(c)
    emit_weight_prep(c)
    final = []
    nseq = SEQ_PER_CORE
    for seq in range(nseq):
        x_loc = "hbm_in"
        for ph in c.phases:
            x_loc = PHASE_FNS[ph](c, seq, x_loc, is_last=(ph == c.phases[-1]))
        final += emit_epilogue(c, seq, x_loc)
    for name in DEBUG_DUMPS:
        if name in c.ws:
            t = c.ws[name]
            dout = nc.dram_tensor("dbg_" + name, list(t.shape), t.dtype,
                                  kind="ExternalOutput")
            tok = c.p.dma(lambda e, t=t, dout=dout: e.dma_start(out=dout.ap(),
                                                                in_=t.ap()),
                          reads=[c.hb[name]], writes=[], tag="dbg")
            final.append(tok)
            continue
        t = c.sb[name]
        dshape = list(t.shape)
        dt = t.dtype
        dout = nc.dram_tensor("dbg_" + name, dshape, dt, kind="ExternalOutput")
        tok = c.p.dma(lambda e, t=t, dout=dout: e.dma_start(out=dout.ap(),
                                                            in_=t[:]),
                      reads=[bf(c, name)], writes=[], tag="dbg")
        final.append(tok)
    final += c.snap_toks
    c.p.finalize(nc, final)
    return nc


def emit_consts(c):
    nc, p = c.nc, c.p
    idn_np = np.eye(128, dtype=np.float32)
    tri_np = np.triu(np.ones((128, 128), dtype=np.float32), 1)
    c.idn_dram = nc.inline_tensor(idn_np, name="idn_c")
    c.tri_dram = nc.inline_tensor(tri_np, name="tri_c")
    sb = c.sb

    p.dma(lambda e: e.dma_start(out=sb["SC"][:, 0:128], in_=c.idn_dram.ap()),
          writes=[bf(c, "SC")], tag="c_idn")
    p.op("dve", lambda e: e.tensor_copy(sb["IDN"][:], sb["SC"][:, 0:128]),
         reads=[bf(c, "SC")], writes=[bf(c, "IDN")], drain=True)
    p.dma(lambda e: e.dma_start(out=sb["SC"][:, 0:128], in_=c.tri_dram.ap()),
          reads=[bf(c, "SC")], writes=[bf(c, "SC")], tag="c_tri")
    p.op("dve", lambda e: e.tensor_copy(sb["TRI"][:], sb["SC"][:, 0:128]),
         reads=[bf(c, "SC")], writes=[bf(c, "TRI")], drain=True)
    p.op("dve", lambda e: e.tensor_scalar_mul(sb["NEGI"][:], sb["IDN"][:], -30000.0),
         reads=[bf(c, "IDN")], writes=[bf(c, "NEGI")], drain=True)
    p.op("pool", lambda e: e.iota(sb["IOTA"][:], [[128, 16]], channel_multiplier=1,
                                  allow_small_or_imprecise_dtypes=True),
         writes=[bf(c, "IOTA")], drain=True)
    p.op("dve", lambda e: e.memset(sb["ONE1"][:], 1.0), writes=[bf(c, "ONE1")], drain=True)
    p.op("dve", lambda e: e.memset(sb["ONES2"][:], 0.0), writes=[bf(c, "ONES2")])
    p.op("dve", lambda e: e.memset(sb["ONES2"][0:64, 0:1], 1.0),
         writes=[bf(c, "ONES2")])
    p.op("dve", lambda e: e.memset(sb["ONES2"][64:128, 1:2], 1.0),
         writes=[bf(c, "ONES2")], drain=True)
    p.op("dve", lambda e: e.memset(sb["CN"][:, 0:1], EPS_LN), writes=[bf(c, "CN")])
    p.op("dve", lambda e: e.memset(sb["ONEC"][:], 1.0), writes=[bf(c, "ONEC")], drain=True)


def fp4_quant(c, src_name, src_ap, dst_ap, dst_buf, n, extra_reads=()):
    p, sb = c.p, c.sb
    nb = n // 16
    srcb = bf(c, src_name)
    r3 = lambda ap: ap.rearrange("p (a b) -> p a b", b=16)
    p.op("dve", lambda e: e.tensor_reduce(sb["SM"][:, 0:nb], r3(src_ap), AX.X,
                                          ALU.max, apply_absolute_value=True),
         reads=[srcb] + list(extra_reads), writes=[bf(c, "SM")], tag="fq_max")
    p.op("dve", lambda e: e.tensor_scalar(sb["SM2"][:, 0:nb], sb["SM"][:, 0:nb],
                                          1.0 / 6.0, 1e-12, ALU.mult, ALU.max),
         reads=[bf(c, "SM")], writes=[bf(c, "SM2")], tag="fq_s", drain=True)
    p.op("dve", lambda e: e.reciprocal(sb["SM"][:, 0:nb], sb["SM2"][:, 0:nb]),
         reads=[bf(c, "SM2")], writes=[bf(c, "SM")], tag="fq_inv")
    inv_bc = sb["SM"][:, 0:nb].rearrange("p a -> p a ()").broadcast_to([128, nb, 16])
    s_bc = sb["SM2"][:, 0:nb].rearrange("p a -> p a ()").broadcast_to([128, nb, 16])
    p.op("dve", lambda e: e.tensor_tensor(r3(sb["SC2"][:, 0:n]), r3(src_ap),
                                          inv_bc, ALU.mult),
         reads=[srcb, bf(c, "SM")] + list(extra_reads), writes=[bf(c, "SC2")],
         tag="fq_xn")
    p.op("dve", lambda e: e.tensor_single_scalar(sb["QI"][:, 0:n],
                                                 sb["SC2"][:, 0:n].bitcast(I32),
                                                 0x7FFFFFFF, ALU.bitwise_and),
         reads=[bf(c, "SC2")], writes=[bf(c, "QI")], tag="fq_i1")
    p.op("dve", lambda e: e.tensor_single_scalar(sb["QI"][:, 0:n], sb["QI"][:, 0:n],
                                                 0x00200000, ALU.add),
         reads=[bf(c, "QI")], writes=[bf(c, "QI")], tag="fq_i2")
    p.op("dve", lambda e: e.tensor_single_scalar(sb["QI2"][:, 0:n], sb["QI"][:, 0:n],
                                                 0xFFC00000 - (1 << 32),
                                                 ALU.bitwise_and),
         reads=[bf(c, "QI")], writes=[bf(c, "QI2")], tag="fq_i3")
    p.op("dve", lambda e: e.tensor_single_scalar(sb["QI2"][:, 0:n], sb["QI2"][:, 0:n],
                                                 0x3F800000, ALU.max),
         reads=[bf(c, "QI2")], writes=[bf(c, "QI2")], tag="fq_i4")
    p.op("act", lambda e: e.activation(sb["SC"][:, 0:n], sb["SC2"][:, 0:n], AF.Sign),
         reads=[bf(c, "SC2")], writes=[bf(c, "SC")], tag="fq_sgn")
    p.op("dve", lambda e: e.tensor_single_scalar(sb["SC2"][:, 0:n], sb["QI"][:, 0:n],
                                                 0x3F600000, ALU.is_gt),
         reads=[bf(c, "QI"), bf(c, "SC2")], writes=[bf(c, "SC2")], tag="fq_i75")
    p.op("dve", lambda e: e.tensor_single_scalar(sb["QI"][:, 0:n].bitcast(F32),
                                                 sb["QI"][:, 0:n], 0x3EA00000,
                                                 ALU.is_gt),
         reads=[bf(c, "QI")], writes=[bf(c, "QI")], tag="fq_i25")
    p.op("dve", lambda e: e.scalar_tensor_tensor(
        sb["QI2"][:, 0:n].bitcast(F32), sb["QI2"][:, 0:n].bitcast(F32), -0.5,
        sb["SC2"][:, 0:n], ALU.add, ALU.mult),
        reads=[bf(c, "QI2"), bf(c, "SC2")], writes=[bf(c, "QI2")], tag="fq_a")
    p.op("dve", lambda e: e.scalar_tensor_tensor(
        sb["SC2"][:, 0:n], sb["QI"][:, 0:n].bitcast(F32), 0.5,
        sb["QI2"][:, 0:n].bitcast(F32), ALU.mult, ALU.add),
        reads=[bf(c, "QI"), bf(c, "QI2"), bf(c, "SC2")], writes=[bf(c, "SC2")],
        tag="fq_b")
    p.op("pool", lambda e: e.tensor_tensor(r3(sb["SC"][:, 0:n]), r3(sb["SC"][:, 0:n]),
                                           s_bc, ALU.mult),
         reads=[bf(c, "SC"), bf(c, "SM2")], writes=[bf(c, "SC")], tag="fq_ss")
    p.op("dve", lambda e: e.tensor_tensor(dst_ap, sb["SC2"][:, 0:n],
                                          sb["SC"][:, 0:n], ALU.mult),
         reads=[bf(c, "SC2"), bf(c, "SC")], writes=[dst_buf], tag="fq_out")


def bcast_scalar(c, src_ap, dst_ap, dst_buf, src_buf):
    p, sb = c.p, c.sb
    p.op("dve", lambda e: e.tensor_copy(sb["BCAST"][0:1, 0:1], src_ap),
         reads=[src_buf], writes=[bf(c, "BCAST")], tag="bs_hi")
    p.op("dve", lambda e: e.tensor_copy(sb["ST"][0:1, 62:63], sb["BCAST"][0:1, 0:1]),
         reads=[bf(c, "BCAST")], writes=[bf(c, "ST")], tag="bs_hif")
    p.op("dve", lambda e: e.tensor_tensor(sb["ST"][0:1, 63:64], src_ap,
                                          sb["ST"][0:1, 62:63], ALU.subtract),
         reads=[src_buf, bf(c, "ST")], writes=[bf(c, "ST")], tag="bs_lo")
    p.op("dve", lambda e: e.tensor_copy(sb["BCAST"][0:1, 1:2], sb["ST"][0:1, 63:64]),
         reads=[bf(c, "ST")], writes=[bf(c, "BCAST")], tag="bs_lob", drain=True)
    p.op("pe", lambda e: e.matmul(c.ps["PO"][:, 1534:1536], sb["ONE1"][:],
                                  sb["BCAST"][0:1, 0:2], start=True, stop=True),
         reads=[bf(c, "ONE1"), bf(c, "BCAST")], writes=pob(c, 1534, 1536), tag="bs_mm")
    p.op("dve", lambda e: e.tensor_copy(sb["SM"][:, 44:46], c.ps["PO"][:, 1534:1536]),
         reads=pob(c, 1534, 1536), writes=[bf(c, "SM")], tag="bs_cp")
    p.op("dve", lambda e: e.tensor_tensor(dst_ap, sb["SM"][:, 44:45],
                                          sb["SM"][:, 45:46], ALU.add),
         reads=[bf(c, "SM")], writes=[dst_buf], tag="bs_add", drain=True)


def _transpose_1024(c, src_ap, src_buf, ncc):
    p = c.p
    for cc in range(ncc):
        p.op("pe", lambda e, cc=cc: e.transpose(
            c.ps["PX"][:, cc * 128:(cc + 1) * 128],
            src_ap[:, cc * 128:(cc + 1) * 128], c.sb["IDN"][:]),
            reads=[src_buf, bf(c, "IDN")], writes=[c.ps_buf["PX"]], tag="tr")


def emit_weight_prep(c):
    nc, p, sb = c.nc, c.p, c.sb
    phases = c.phases
    casts = []
    if 0 in phases:
        casts += [("L0_attn_q_w", "qb"), ("L0_attn_k_w", "kb"),
                  ("L0_attn_v_w", "vb"), ("L0_attn_o_w", "ob")]
    if 2 in phases:
        casts += [("L1_attn_Wq", "Wqb"), ("L1_attn_Wk", "Wkb"),
                  ("L1_attn_Wv", "Wvb"), ("L1_attn_Wo", "Wob")]
    pp = [0]
    for src, dst in casts:
        for t in range(NC):
            sl = slice(t * 128, (t + 1) * 128)
            wn = ("WB", "WB2")[pp[0] % 2]
            qn = ("QOUT", "QOUT2")[pp[0] % 2]
            pp[0] += 1
            p.dma(lambda e, src=src, sl=sl, wn=wn: e.dma_start(
                out=sb[wn][:], in_=c.w_in[src].ap()[sl, :]),
                reads=[bf(c, wn)], writes=[bf(c, wn)], tag="wp_ld")
            p.op("act", lambda e, wn=wn, qn=qn: e.activation(
                sb[qn][:], sb[wn][:], AF.Copy),
                reads=[bf(c, wn)], writes=[bf(c, qn)], tag="wp_cast")
            p.dma(lambda e, dst=dst, sl=sl, qn=qn: e.dma_start(
                out=c.ws[dst].ap()[sl, :], in_=sb[qn][:]),
                reads=[bf(c, qn)], writes=[c.hb[dst]], tag="wp_st")
    if 2 in phases:
        p.dma(lambda e: e.dma_start(out=sb["WB"][0:64, 0:M1],
                                    in_=c.w_in["L1_attn_random_matrix"].ap()),
              reads=[bf(c, "WB")], writes=[bf(c, "WB")])
        p.op("dve", lambda e: e.tensor_copy(sb["QOUT"][0:64, 0:M1],
                                            sb["WB"][0:64, 0:M1]),
             reads=[bf(c, "WB")], writes=[bf(c, "QOUT")])
        p.dma(lambda e: e.dma_start(out=c.ws["RMb"].ap(), in_=sb["QOUT"][0:64, 0:M1]),
              reads=[bf(c, "QOUT")], writes=[c.hb["RMb"]])
    if 1 in phases:
        for src, dst, rows, cols in [("L0_ffn_w_gate", "wgT0", I0, D),
                                     ("L0_ffn_w_up", "wuT0", I0, D),
                                     ("L0_ffn_w_down", "wdT0", D, I0)]:
            emit_fp4_weight_prep(c, src, dst, rows, cols)
    if 3 in phases:
        for src, dst, rows, cols in [("L1_ffn_w_gate", "wgT1", I1, D),
                                     ("L1_ffn_w_up", "wuT1", I1, D),
                                     ("L1_ffn_w_down", "wdT1", D, I1)]:
            emit_bitnet_weight_prep(c, src, dst, rows, cols)


def emit_fp4_weight_prep(c, src, dst, rows, cols):
    p, sb = c.p, c.sb
    ck = [0]
    for t in range(rows // 128):
        sl = slice(t * 128, (t + 1) * 128)
        for c0 in range(0, cols, 1024):
            cw = min(1024, cols - c0)
            wn = ("WB", "WB2")[ck[0] % 2]
            qn = ("QOUT", "QOUT2")[ck[0] % 2]
            hn = ("HQT", "HQT2")[ck[0] % 2]
            ck[0] += 1
            p.dma(lambda e, sl=sl, c0=c0, cw=cw, wn=wn: e.dma_start(
                out=sb[wn][:, 0:cw], in_=c.w_in[src].ap()[sl, c0:c0 + cw]),
                reads=[bf(c, wn)], writes=[bf(c, wn)], tag="f4w_ld")
            fp4_quant(c, wn, sb[wn][:, 0:cw], sb[qn][:, 0:cw],
                      bf(c, qn), cw)
            ncc = cw // 128
            _transpose_1024(c, sb[qn][:], bf(c, qn), ncc)
            p.op("act", lambda e, ncc=ncc, hn=hn: e.activation(
                sb[hn][:].rearrange("p a b -> p (a b)")[:, 0:ncc * 128],
                c.ps["PX"][:, 0:ncc * 128], AF.Copy),
                reads=[c.ps_buf["PX"]], writes=[bf(c, hn)], tag="f4w_ev")
            p.dma(lambda e, t=t, c0=c0, ncc=ncc, hn=hn: e.dma_start(
                out=c.ws[dst].ap().rearrange("(a q) r -> q a r", q=128)
                    [:, c0 // 128:c0 // 128 + ncc, t * 128:(t + 1) * 128],
                in_=sb[hn][:, 0:ncc, :]),
                reads=[bf(c, hn)], writes=[c.hb[dst]], tag="f4w_st")


def emit_bitnet_weight_prep(c, src, dst, rows, cols):
    p, sb = c.p, c.sb
    col = {"wgT1": 1, "wuT1": 2, "wdT1": 3}[dst]
    nrt = rows // 128
    p.op("dve", lambda e: e.memset(sb["ST"][:, 0:1], 0.0), writes=[bf(c, "ST")],
         tag="bnw_z")
    ck = [0]
    for t in range(nrt):
        sl = slice(t * 128, (t + 1) * 128)
        for c0 in range(0, cols, 1024):
            cw = min(1024, cols - c0)
            wn = ("WB", "WB2")[ck[0] % 2]
            ck[0] += 1
            p.dma(lambda e, sl=sl, c0=c0, cw=cw, wn=wn: e.dma_start(
                out=sb[wn][:, 0:cw], in_=c.w_in[src].ap()[sl, c0:c0 + cw]),
                reads=[bf(c, wn)], writes=[bf(c, wn)], tag="bnw_ld")
            p.op("dve", lambda e, cw=cw, wn=wn: e.tensor_reduce(
                sb["ST"][:, 1:2], sb[wn][:, 0:cw], AX.X, ALU.add,
                apply_absolute_value=True),
                reads=[bf(c, wn)], writes=[bf(c, "ST")], tag="bnw_red")
            p.op("dve", lambda e: e.tensor_tensor(sb["ST"][:, 0:1], sb["ST"][:, 0:1],
                                                  sb["ST"][:, 1:2], ALU.add),
                 reads=[bf(c, "ST")], writes=[bf(c, "ST")], tag="bnw_acc")
    p.op("dve", lambda e: e.tensor_scalar_mul(sb["SM"][:, 40:41], sb["ST"][:, 0:1],
                                              1.0 / (rows * cols)),
         reads=[bf(c, "ST")], writes=[bf(c, "SM")], tag="bnw_c")
    p.op("dve", lambda e: e.tensor_copy(sb["BCAST"][0:1, 2:3], sb["SM"][0:1, 40:41])
         if False else e.tensor_copy(sb["SM"][:, 41:42].bitcast(BF16)[:, 1:2],
                                     sb["SM"][:, 40:41]),
         reads=[bf(c, "SM")], writes=[bf(c, "SM")], tag="bnw_cb16", drain=True)
    p.op("pe", lambda e: e.matmul(c.ps["PO"][0:1, 256:257],
                                  sb["SM"][:, 41:42].bitcast(BF16)[:, 1:2],
                                  sb["ONEC"][:], start=True, stop=True),
         reads=[bf(c, "SM"), bf(c, "ONEC")], writes=pob(c, 256, 257), tag="bnw_mm")
    p.op("dve", lambda e: e.tensor_scalar_max(sb["ST"][0:1, 4:5],
                                              c.ps["PO"][0:1, 256:257], 1e-5),
         reads=pob(c, 256, 257), writes=[bf(c, "ST")], tag="bnw_mean")
    p.op("dve", lambda e: e.reciprocal(sb["ST"][0:1, 5:6], sb["ST"][0:1, 4:5]),
         reads=[bf(c, "ST")], writes=[bf(c, "ST")], tag="bnw_rcp")
    bcast_scalar(c, sb["ST"][0:1, 5:6], sb["SM"][:, 8:9], bf(c, "SM"), bf(c, "ST"))
    bcast_scalar(c, sb["ST"][0:1, 4:5], sb["CN"][:, col:col + 1], bf(c, "CN"),
                 bf(c, "ST"))
    ck = [0]
    for t in range(nrt):
        sl = slice(t * 128, (t + 1) * 128)
        for c0 in range(0, cols, 1024):
            cw = min(1024, cols - c0)
            wn = ("WB", "WB2")[ck[0] % 2]
            qn = ("QOUT", "QOUT2")[ck[0] % 2]
            hn = ("HQT", "HQT2")[ck[0] % 2]
            ck[0] += 1
            p.dma(lambda e, sl=sl, c0=c0, cw=cw, wn=wn: e.dma_start(
                out=sb[wn][:, 0:cw], in_=c.w_in[src].ap()[sl, c0:c0 + cw]),
                reads=[bf(c, wn)], writes=[bf(c, wn)], tag="bnw_ld2")
            p.op("dve", lambda e, cw=cw, wn=wn: e.tensor_single_scalar(
                sb["SC2"][:, 0:cw], sb[wn][:, 0:cw], sb["SM"][:, 8:9], ALU.mult),
                reads=[bf(c, wn), bf(c, "SM")], writes=[bf(c, "SC2")],
                tag="bnw_m0")
            p.op("dve", lambda e, cw=cw: e.tensor_scalar(
                sb["SC2"][:, 0:cw], sb["SC2"][:, 0:cw],
                -1.0, 1.0, ALU.max, ALU.min),
                reads=[bf(c, "SC2")], writes=[bf(c, "SC2")], tag="bnw_clf")
            p.op("dve", lambda e, cw=cw: e.tensor_scalar_add(
                sb["SC2"][:, 0:cw], sb["SC2"][:, 0:cw], MAGIC),
                reads=[bf(c, "SC2")], writes=[bf(c, "SC2")], tag="bnw_m")
            p.op("dve", lambda e, cw=cw: e.tensor_single_scalar(
                sb["QI"][:, 0:cw], sb["SC2"][:, 0:cw].bitcast(I32),
                0, ALU.bitwise_or),
                reads=[bf(c, "SC2")], writes=[bf(c, "QI")], tag="bnw_cl")
            p.op("dve", lambda e, cw=cw, qn=qn: e.tensor_scalar_add(
                sb[qn][:, 0:cw], sb["QI"][:, 0:cw].bitcast(F32), -MAGIC),
                reads=[bf(c, "QI")], writes=[bf(c, qn)], tag="bnw_b")
            ncc = cw // 128
            _transpose_1024(c, sb[qn][:], bf(c, qn), ncc)
            p.op("act", lambda e, ncc=ncc, hn=hn: e.activation(
                sb[hn][:].rearrange("p a b -> p (a b)")[:, 0:ncc * 128],
                c.ps["PX"][:, 0:ncc * 128], AF.Copy),
                reads=[c.ps_buf["PX"]], writes=[bf(c, hn)], tag="bnw_ev")
            p.dma(lambda e, t=t, c0=c0, ncc=ncc, hn=hn: e.dma_start(
                out=c.ws[dst].ap().rearrange("(a q) r -> q a r", q=128)
                    [:, c0 // 128:c0 // 128 + ncc, t * 128:(t + 1) * 128],
                in_=sb[hn][:, 0:ncc, :]),
                reads=[bf(c, hn)], writes=[c.hb[dst]], tag="bnw_st")


def emit_fast_rsqrt(c, dst_ap, src_ap, n, scratch_off=48):
    p, sb = c.p, c.sb
    so = scratch_off
    p.op("dve", lambda e: e.tensor_single_scalar(sb["QI"][:, 0:n],
                                                 src_ap.bitcast(I32), 1,
                                                 ALU.arith_shift_right),
         reads=[], writes=[bf(c, "QI")], tag="rs_sh0")
    p.op("dve", lambda e: e.tensor_single_scalar(sb["QI"][:, 0:n], sb["QI"][:, 0:n],
                                                 0x7FFFFFFF, ALU.bitwise_xor),
         reads=[bf(c, "QI")], writes=[bf(c, "QI")], tag="rs_x")
    p.op("dve", lambda e: e.tensor_single_scalar(sb["QI"][:, 0:n], sb["QI"][:, 0:n],
                                                 0x5F3759DF - 0x7FFFFFFF, ALU.add),
         reads=[bf(c, "QI")], writes=[bf(c, "QI")], tag="rs_a")
    y = sb["QI"][:, 0:n].bitcast(F32)
    for it in range(2):
        p.op("dve", lambda e: e.tensor_tensor(sb["ST2"][:, so:so + n], y, y,
                                              ALU.mult),
             reads=[bf(c, "QI")], writes=[bf(c, "ST2")], tag="rs_y2")
        p.op("dve", lambda e: e.tensor_tensor(sb["ST2"][:, so:so + n],
                                              sb["ST2"][:, so:so + n], src_ap,
                                              ALU.mult),
             reads=[bf(c, "ST2")], writes=[bf(c, "ST2")], tag="rs_vy2")
        p.op("dve", lambda e: e.tensor_scalar(sb["ST2"][:, so:so + n],
                                              sb["ST2"][:, so:so + n], -0.5, 1.5,
                                              ALU.mult, ALU.add),
             reads=[bf(c, "ST2")], writes=[bf(c, "ST2")], tag="rs_c")
        p.op("dve", lambda e: e.tensor_tensor(y, y, sb["ST2"][:, so:so + n],
                                              ALU.mult),
             reads=[bf(c, "ST2"), bf(c, "QI")], writes=[bf(c, "QI")], tag="rs_y")
    p.op("dve", lambda e: e.tensor_copy(dst_ap, y),
         reads=[bf(c, "QI")], writes=[bf(c, "ST")], tag="rs_out")


def emit_ln_stats(c, src_tile_fn, reads_fn):
    p, sb = c.p, c.sb
    for t in range(NT):
        p.op("dve", lambda e, t=t: e.reduce_sum(sb["ST"][:, t:t + 1],
                                                src_tile_fn(t), AX.X),
             reads=reads_fn(t), writes=[bf(c, "ST")], tag="ln_sum")
        p.op("dve", lambda e, t=t: e.scalar_tensor_tensor(
            sb["SC2"][:], src_tile_fn(t), 0.0, src_tile_fn(t), ALU.bypass,
            ALU.mult),
            reads=reads_fn(t), writes=[bf(c, "SC2")], tag="ln_sq")
        p.op("dve", lambda e, t=t: e.reduce_sum(sb["ST"][:, 32 + t:33 + t],
                                                sb["SC2"][:], AX.X),
             reads=[bf(c, "SC2")], writes=[bf(c, "ST")], tag="ln_sqr")
    p.op("dve", lambda e: e.tensor_scalar_mul(sb["ST"][:, 0:16], sb["ST"][:, 0:16],
                                              1.0 / D),
         reads=[bf(c, "ST")], writes=[bf(c, "ST")], tag="ln_mean")
    p.op("dve", lambda e: e.tensor_scalar_mul(sb["ST2"][:, 16:32], sb["ST"][:, 32:48],
                                              1.0 / D),
         reads=[bf(c, "ST")], writes=[bf(c, "ST2")], tag="ln_e2")
    p.op("dve", lambda e: e.tensor_tensor(sb["ST2"][:, 32:48], sb["ST"][:, 0:16],
                                          sb["ST"][:, 0:16], ALU.mult),
         reads=[bf(c, "ST")], writes=[bf(c, "ST2")], tag="ln_m2")
    p.op("dve", lambda e: e.tensor_tensor(sb["ST2"][:, 16:32], sb["ST2"][:, 16:32],
                                          sb["ST2"][:, 32:48], ALU.subtract),
         reads=[bf(c, "ST2")], writes=[bf(c, "ST2")], tag="ln_var")
    p.op("dve", lambda e: e.tensor_scalar_add(sb["ST2"][:, 16:32],
                                              sb["ST2"][:, 16:32], EPS_LN),
         reads=[bf(c, "ST2")], writes=[bf(c, "ST2")], tag="ln_vep")
    emit_fast_rsqrt(c, sb["ST"][:, 16:32], sb["ST2"][:, 16:32], 16)
    p.op("dve", lambda e: e.scalar_tensor_tensor(sb["ST2"][:, 0:16], sb["ST"][:, 0:16],
                                                 -1.0, sb["ST"][:, 16:32],
                                                 ALU.mult, ALU.mult),
         reads=[bf(c, "ST")], writes=[bf(c, "ST2")], tag="ln_nmr")


def emit_ln_apply(c, t, src_ap, dst_ap, reads, writes):
    p, sb = c.p, c.sb
    p.op("dve", lambda e: e.tensor_scalar(dst_ap, src_ap, sb["ST"][:, 16 + t:17 + t],
                                          sb["ST2"][:, t:t + 1], ALU.mult, ALU.add),
         reads=list(reads) + [bf(c, "ST"), bf(c, "ST2")], writes=list(writes),
         tag="ln_app")


def emit_xt_from_bf16_tiles(c, t, src_ap, src_buf):
    p, sb = c.p, c.sb
    _transpose_1024(c, src_ap, src_buf, 8)
    p.op("act", lambda e, t=t: e.activation(
        sb["XT"][:, :, t * 128:(t + 1) * 128],
        c.ps["PX"][:].rearrange("p (a b) -> p a b", b=128), AF.Copy),
        reads=[c.ps_buf["PX"]], writes=[bf(c, "XT")], tag="xt_ev")


def emit_ln_to_xt(c, reads_extra=()):
    sb = c.sb
    emit_ln_stats(c, lambda t: sb["X"][:, t, :], lambda t: [bf(c, "X")])
    dbg_snap(c, "ln_st", "ST")
    dbg_snap(c, "ln_st2", "ST2")
    for t in range(NT):
        emit_ln_apply(c, t, sb["X"][:, t, :], sb["QOUT"][:],
                      reads=[bf(c, "X")], writes=[bf(c, "QOUT")])
        if t == 8:
            dbg_snap(c, "qout_t8", "QOUT")
        emit_xt_from_bf16_tiles(c, t, sb["QOUT"][:], bf(c, "QOUT"))


def emit_load_x(c, seq, x_loc):
    p, sb = c.p, c.sb
    if x_loc == "hbm_in":
        src = c.x_in.ap()[seq].rearrange("(t p) d -> p t d", p=128)
        hbuf = c.hb["x"]
    else:
        src = c.xs.ap().rearrange("(t p) d -> p t d", p=128)
        hbuf = c.hb["xs"]
    p.dma(lambda e: e.dma_start(out=sb["X"][:], in_=src),
          reads=[hbuf], writes=[bf(c, "X")], tag="x_ld")


def _load_wslice(c, ws_name, pair, dst_name):
    p, sb = c.p, c.sb
    src = c.ws[ws_name].ap().rearrange("(a q) r -> q a r", q=128)[
        :, :, pair * 128:(pair + 1) * 128]
    p.dma(lambda e: e.dma_start(out=sb[dst_name][:], in_=src),
          reads=[c.hb[ws_name], bf(c, dst_name)], writes=[bf(c, dst_name)],
          tag="wsl")


def _proj_fm(c, wbuf_name, dst_name, dst_buf, evac_eng, scale=1.0, bias_col=None,
             extra_evac_reads=()):
    p, sb = c.p, c.sb
    for qc in range(4):
        quarter = qc % 2
        qsl = slice(quarter * 512, (quarter + 1) * 512)
        for kc in range(NC):
            p.op("pe", lambda e, qc=qc, kc=kc, qsl=qsl: e.matmul(
                c.ps["PSS"][:, qsl], sb[wbuf_name][:, kc, :],
                sb["XT"][:, kc, qc * 512:(qc + 1) * 512],
                start=(kc == 0), stop=(kc == NC - 1)),
                reads=[bf(c, wbuf_name), bf(c, "XT")],
                writes=[c.ps_buf["PSS%d" % quarter]], tag="prj_mm")
        if bias_col is None:
            em = lambda e, qc=qc, qsl=qsl: e.activation(
                sb[dst_name][:, qc * 512:(qc + 1) * 512], c.ps["PSS"][:, qsl],
                AF.Copy, scale=scale)
            eng = evac_eng
        else:
            em = lambda e, qc=qc, qsl=qsl: e.tensor_scalar_add(
                sb[dst_name][:, qc * 512:(qc + 1) * 512], c.ps["PSS"][:, qsl],
                bias_col)
            eng = "dve"
        p.op(eng, em,
             reads=[c.ps_buf["PSS%d" % (qc % 2)]] + list(extra_evac_reads),
             writes=[dst_buf], tag="prj_ev")


def _proj_v_tokmajor(c, wbuf_name, pair, bias_col=None):
    p, sb = c.p, c.sb
    _proj_fm(c, wbuf_name, "PT", bf(c, "PT"), "act", bias_col=bias_col,
             extra_evac_reads=[bf(c, "BCOL")] if bias_col is not None else ())
    if P2_TRUNC == 15:
        return
    for g in range(2):
        for j in range(8):
            t = g * 8 + j
            p.op("pe", lambda e, t=t, j=j: e.transpose(
                c.ps["PX"][:, j * 128:(j + 1) * 128],
                sb["PT"][:, t * 128:(t + 1) * 128], sb["IDN"][:]),
                reads=[bf(c, "PT"), bf(c, "IDN")], writes=[c.ps_buf["PX"]],
                tag="v_tr")
        p.op("dve", lambda e, g=g: e.tensor_copy(
            sb["VP"][:, :, g * 8:(g + 1) * 8, 0:HD],
            c.ps["PX"][:].rearrange("p (t h d) -> p h t d", t=8, h=2),
            ),
            reads=[c.ps_buf["PX"]], writes=[bf(c, "VP")], tag="v_ev")
    p.op("dve", lambda e: e.memset(sb["VP"][:, :, :, HD:HD + 1], 1.0),
         writes=[bf(c, "VP")], tag="v_one", drain=True)


def emit_p0(c, seq, x_loc, is_last):
    p, sb = c.p, c.sb
    claim(c, "X", "OW", "QP", "KP", "VP", "WS1", "WS2", "PT", "NIO", "QH", "KH",
          "ROWSC")
    p.op("pool", lambda e: e.iota(sb["NIO"][64:65, :], [[-1, S]],
                                  channel_multiplier=0,
                                  allow_small_or_imprecise_dtypes=True),
         reads=[bf(c, "NIO")], writes=[bf(c, "NIO")], tag="c_nio", drain=True)
    emit_load_x(c, seq, x_loc)
    emit_ln_to_xt(c)
    dbg_snap(c, "xt_ln", "XT")
    p.dma(lambda e: e.dma_start(
        out=sb["OW"][:], in_=c.ws["ob"].ap().rearrange("(a q) r -> q a r", q=128)),
        reads=[c.hb["ob"], bf(c, "OW")], writes=[bf(c, "OW")], tag="ow_ld")
    for pair in range(NPAIR):
        _load_wslice(c, "qb", pair, "WS1")
        if pair == 0:
            dbg_snap(c, "ws1_p0", "WS1")
        _proj_fm(c, "WS1", "QP", bf(c, "QP"), "act", scale=0.125)
        if pair == 0:
            dbg_snap(c, "qp_p0", "QP")
        _load_wslice(c, "kb", pair, "WS2")
        _proj_fm(c, "WS2", "KP", bf(c, "KP"), "act")
        _load_wslice(c, "vb", pair, "WS1")
        _proj_v_tokmajor(c, "WS1", pair)
        emit_pair_norm_bound(c)
        for half in (1, 0):
            emit_head_attention(c, seq, pair, half)
    emit_out_proj(c, target="X", bias_row=None)
    return "X"


def emit_pair_norm_bound(c):
    p, sb = c.p, c.sb
    for j, buf in enumerate(("QP", "KP")):
        p.op("act", lambda e, buf=buf: e.activation(sb["PT"][:], sb[buf][:],
                                                    AF.Square),
             reads=[bf(c, buf)], writes=[bf(c, "PT")], tag="nb_sq")
        for ch in range(4):
            p.op("pe", lambda e, ch=ch: e.matmul(
                c.ps["PSS"][0:2, ch * 512:(ch + 1) * 512],
                sb["ONES2"][:], sb["PT"][:, ch * 512:(ch + 1) * 512],
                start=True, stop=True),
                reads=[bf(c, "ONES2"), bf(c, "PT")],
                writes=[c.ps_buf["PSS%d" % ch]], tag="nb_mm")
        p.op("dve", lambda e, j=j: e.tensor_reduce(
            sb["NRM"][0:2, j:j + 1], c.ps["PSS"][0:2, :], AX.X, ALU.max),
            reads=[c.ps_buf["PSS%d" % q] for q in range(4)],
            writes=[bf(c, "NRM")], tag="nb_red")
    p.op("dve", lambda e: e.tensor_tensor(sb["NRM"][0:2, 2:3], sb["NRM"][0:2, 0:1],
                                          sb["NRM"][0:2, 1:2], ALU.mult),
         reads=[bf(c, "NRM")], writes=[bf(c, "NRM")], tag="nb_p")
    p.op("dve", lambda e: e.tensor_copy(sb["MCOL"][0:2, 0:1], sb["NRM"][0:2, 2:3]),
         reads=[bf(c, "NRM")], writes=[bf(c, "MCOL")], tag="nb_cb", drain=True)
    p.op("pe", lambda e: e.transpose(c.ps["PX"][0:1, 0:2], sb["MCOL"][0:2, 0:1],
                                     sb["IDN"][0:2, 0:2]),
         reads=[bf(c, "MCOL"), bf(c, "IDN")], writes=[c.ps_buf["PX"]], tag="nb_tr")
    p.op("dve", lambda e: e.tensor_reduce(sb["NRM"][0:1, 4:5], c.ps["PX"][0:1, 0:2],
                                          AX.X, ALU.max),
         reads=[c.ps_buf["PX"]], writes=[bf(c, "NRM")], tag="nb_mx")
    p.op("act", lambda e: e.activation(sb["NRM"][0:1, 5:6], sb["NRM"][0:1, 4:5],
                                       AF.Sqrt),
         reads=[bf(c, "NRM")], writes=[bf(c, "NRM")], tag="nb_sqrt")
    p.op("dve", lambda e: e.tensor_scalar(sb["NRM"][0:1, 6:7], sb["NRM"][0:1, 5:6],
                                          1.0, 8.0, ALU.mult, ALU.add),
         reads=[bf(c, "NRM")], writes=[bf(c, "NRM")], tag="nb_m")
    p.op("dve", lambda e: e.tensor_copy(sb["MCOL"][0:1, 1:2], sb["NRM"][0:1, 6:7]),
         reads=[bf(c, "NRM")], writes=[bf(c, "MCOL")], tag="nb_cb2", drain=True)
    p.op("pe", lambda e: e.matmul(c.ps["PO"][:, 256:257], sb["ONE1"][:],
                                  sb["MCOL"][0:1, 1:2], start=True, stop=True),
         reads=[bf(c, "MCOL"), bf(c, "ONE1")], writes=pob(c, 256, 257), tag="nb_bc")
    p.op("dve", lambda e: e.tensor_scalar_mul(sb["MB"][:], c.ps["PO"][:, 256:257],
                                              -1.0),
         reads=pob(c, 256, 257), writes=[bf(c, "MB")], tag="nb_neg")


def emit_head_attention(c, seq, pair, half):
    p, sb = c.p, c.sb
    h = 2 * pair + half
    slope = SLOPES[h]
    import numpy as _np
    import ml_dtypes as _md
    s_hi = float(_np.float32(_md.bfloat16(slope)))
    s_lo = float(_np.float32(_md.bfloat16(slope - s_hi)))
    if half == 1:
        for dst, srcname in (("QH", "QP"), ("KH", "KP")):
            p.dma(lambda e, dst=dst, srcname=srcname: e.dma_start(
                out=sb[dst][0:64, :], in_=sb[srcname][64:128, :]),
                reads=[bf(c, srcname), bf(c, dst)], writes=[bf(c, dst)],
                tag="ah_cp")
        qhat_n, khat_n = "QH", "KH"
    else:
        qhat_n, khat_n = "QP", "KP"
    qhat, khat = sb[qhat_n], sb[khat_n]
    qb_, kb_ = bf(c, qhat_n), bf(c, khat_n)
    p.op("dve", lambda e: e.tensor_copy(qhat[64:65, :], sb["NIO"][64:65, :]),
         reads=[bf(c, "NIO"), qb_], writes=[qb_], tag="ah_q64")
    p.op("dve", lambda e: e.memset(khat[64:65, :], slope),
         reads=[kb_], writes=[kb_], tag="ah_k64", drain=True)
    p.op("pool", lambda e: e.iota(
        sb["ROWSC"][0:1, :].rearrange("o (a b) -> o a b", b=128),
        [[128, 16], [0, 128]], channel_multiplier=0,
        allow_small_or_imprecise_dtypes=True),
        reads=[bf(c, "ROWSC")], writes=[bf(c, "ROWSC")], tag="ah_mk")
    p.op("dve", lambda e: e.tensor_scalar_add(sb["ROWSC"][0:1, :],
                                              sb["ROWSC"][0:1, :], 64.0),
         reads=[bf(c, "ROWSC")], writes=[bf(c, "ROWSC")], tag="ah_mk2",
         drain=True)
    for rr in (65, 66):
        p.dma(lambda e, rr=rr: e.dma_start(out=khat[rr:rr + 1, :],
                                           in_=sb["ROWSC"][0:1, :]),
              reads=[bf(c, "ROWSC"), kb_], writes=[kb_], tag="ah_krow")
    for rr, val in ((65, s_hi), (66, s_lo)):
        p.op("dve", lambda e, val=val: e.memset(sb["ROWSC"][0:1, :], val),
             reads=[bf(c, "ROWSC")], writes=[bf(c, "ROWSC")], tag="ah_sv",
             drain=True)
        p.dma(lambda e, rr=rr: e.dma_start(out=qhat[rr:rr + 1, :],
                                           in_=sb["ROWSC"][0:1, :]),
              reads=[bf(c, "ROWSC"), qb_], writes=[qb_], tag="ah_qrow")
    p.op("dve", lambda e: e.tensor_scalar(sb["BIAS"][:, 0:1], sb["IOTA"][:, 0:1],
                                          slope, -64.0 * slope, ALU.mult,
                                          ALU.add),
         reads=[bf(c, "IOTA")], writes=[bf(c, "BIAS")], tag="at_bias0")
    p.op("dve", lambda e: e.tensor_tensor(sb["BIAS"][:, 0:1], sb["BIAS"][:, 0:1],
                                          sb["MB"][:], ALU.add),
         reads=[bf(c, "BIAS"), bf(c, "MB")], writes=[bf(c, "BIAS")],
         tag="at_bias", drain=True)
    def ptn_of(it):
        return "PT" if it % 2 == 0 else "PT2"

    def emit_pv(jt):
        ptn = ptn_of(jt)
        for kb in range(jt + 1):
            p.op("pe", lambda e, kb=kb, jt=jt, half=half, ptn=ptn: e.matmul(
                c.ps["PO"][:, po_off(jt):po_off(jt) + 65],
                sb[ptn][:, kb * 128:(kb + 1) * 128],
                sb["VP"][:, half, kb, :],
                start=(kb == 0), stop=(kb == jt)),
                reads=[bf(c, ptn), bf(c, "VP")],
                writes=pob(c, po_off(jt), po_off(jt) + 65), tag="at_pv")

    for it in range(NT):
        ptn = ptn_of(it)
        for kb in range(it + 1):
            wbuf = [c.ps_buf["PSS%d" % (kb // 4)]]
            p.op("pe", lambda e, kb=kb, it=it: e.matmul(
                c.ps["PSS"][:, kb * 128:(kb + 1) * 128],
                khat[0:67, kb * 128:(kb + 1) * 128],
                qhat[0:67, it * 128:(it + 1) * 128],
                start=True, stop=(kb != it)),
                reads=[kb_, qb_], writes=wbuf, tag="at_qk")
        p.op("pe", lambda e, it=it: e.matmul(
            c.ps["PSS"][:, it * 128:(it + 1) * 128], sb["TRI"][:], sb["NEGI"][:],
            start=False, stop=True),
            reads=[bf(c, "TRI"), bf(c, "NEGI")],
            writes=[c.ps_buf["PSS%d" % (it // 4)]], tag="at_tri")
        for b in range(it // 4 + 1):
            c0 = b * 512
            c1 = min((it + 1) * 128, c0 + 512)
            p.op("act", lambda e, c0=c0, c1=c1, ptn=ptn: e.activation(
                sb[ptn][:, c0:c1], c.ps["PSS"][:, c0:c1],
                AF.Exp, bias=sb["BIAS"][:, 0:1]),
                reads=[c.ps_buf["PSS%d" % b], bf(c, "BIAS")],
                writes=[bf(c, ptn)], tag="at_exp")
        if it > 0:
            emit_pv(it - 1)
        if it - 1 == 6:
            emit_attn_norm_evac(c, h, groups=[0])
        if it - 1 == 13:
            emit_attn_norm_evac(c, h, groups=[1])
    emit_pv(NT - 1)
    emit_attn_norm_evac(c, h, groups=[2])


def emit_attn_norm_evac(c, h, clamp=None, groups=None):
    p, sb = c.p, c.sb
    allg = [(0, 7), (7, 14), (14, 16)]
    sel = allg if groups is None else [allg[g] for g in groups]
    for (i0, i1) in sel:
        n = i1 - i0
        sums = c.ps["PO"][:, po_off(i0):po_off(i0) + n * 65]\
            .rearrange("p (t u) -> p t u", u=65)[:, :, 64:65]\
            .rearrange("p t o -> p (t o)")
        if clamp is None:
            p.op("dve", lambda e, i0=i0, n=n, sums=sums: e.reciprocal(
                sb["ST2"][:, 48 + i0:48 + i0 + n], sums),
                reads=pob(c, po_off(i0), po_off(i0) + n * 65),
                writes=[bf(c, "ST2")], tag="at_rcp")
        else:
            p.op("dve", lambda e, i0=i0, n=n, sums=sums: e.tensor_scalar_max(
                sb["ST2"][:, 48 + i0:48 + i0 + n], sums, clamp),
                reads=pob(c, po_off(i0), po_off(i0) + n * 65),
                writes=[bf(c, "ST2")], tag="at_cl")
            p.op("dve", lambda e, i0=i0, n=n: e.reciprocal(
                sb["ST2"][:, 48 + i0:48 + i0 + n],
                sb["ST2"][:, 48 + i0:48 + i0 + n]),
                reads=[bf(c, "ST2")], writes=[bf(c, "ST2")], tag="at_rcp")
        p.op("dve", lambda e, i0=i0, n=n, h=h: e.tensor_tensor(
            sb["BB"][:, i0:i0 + n, h * 64:(h + 1) * 64],
            c.ps["PO"][:, po_off(i0):po_off(i0) + n * 65]
            .rearrange("p (t u) -> p t u", u=65)[:, :, 0:64],
            sb["ST2"][:, 48 + i0:48 + i0 + n].rearrange("p t -> p t ()")
            .broadcast_to([128, n, 64]), ALU.mult),
            reads=pob(c, po_off(i0), po_off(i0) + n * 65) + [bf(c, "ST2")],
            writes=[bf(c, "BB")], tag="at_nrm", drain=True)


def emit_out_proj(c, target, bias_row):
    p, sb = c.p, c.sb
    for t in range(NT):
        emit_xt_from_bf16_tiles(c, t, sb["BB"][:, t, :], bf(c, "BB"))
    for t in range(NT):
        for nh in range(2):
            qsl = slice(nh * 512, nh * 512 + 512)
            for kc in range(NC):
                p.op("pe", lambda e, t=t, kc=kc, nh=nh, qsl=qsl: e.matmul(
                    c.ps["PSS"][:, qsl], sb["XT"][:, kc, t * 128:(t + 1) * 128],
                    sb["OW"][:, kc, nh * 512:(nh + 1) * 512],
                    start=(kc == 0), stop=(kc == NC - 1 and bias_row is None)),
                    reads=[bf(c, "XT"), bf(c, "OW")],
                    writes=[c.ps_buf["PSS%d" % nh]], tag="op_mm")
            if bias_row is not None:
                p.op("pe", lambda e, nh=nh, qsl=qsl: e.matmul(
                    c.ps["PSS"][:, qsl], sb["ONE1"][:],
                    sb["ROWB"][0:1, nh * 512:(nh + 1) * 512],
                    start=False, stop=True),
                    reads=[bf(c, "ONE1"), bf(c, "ROWB")],
                    writes=[c.ps_buf["PSS%d" % nh]], tag="op_bias")
        p.op("dve", lambda e, t=t: e.tensor_tensor(
            sb["X"][:, t, :], c.ps["PSS"][:, 0:1024], sb["X"][:, t, :], ALU.add),
            reads=[c.ps_buf["PSS0"], c.ps_buf["PSS1"], bf(c, "X")],
            writes=[bf(c, "X")], tag="op_res")


def emit_p1(c, seq, x_loc, is_last):
    p, sb = c.p, c.sb
    claim(c, "X", "PT")
    if x_loc != "X":
        emit_load_x(c, seq, x_loc)
    emit_ln_stats(c, lambda t: sb["X"][:, t, :], lambda t: [bf(c, "X")])
    for t in range(NT):
        emit_ln_apply(c, t, sb["X"][:, t, :], sb["SC"][:],
                      reads=[bf(c, "X")], writes=[bf(c, "SC")])
        fp4_quant(c, "SC", sb["SC"][:], sb["QOUT"][:], bf(c, "QOUT"), 1024)
        if DEBUG_SNAP and t == 0 and seq == 0:
            dbg_snap(c, "xq_t0", "QOUT")
        emit_xt_from_bf16_tiles(c, t, sb["QOUT"][:], bf(c, "QOUT"))
    p.dma(lambda e: e.dma_start(
        out=c.xs.ap().rearrange("(t p) d -> p t d", p=128), in_=sb["X"][:]),
        reads=[bf(c, "X")], writes=[c.hb["xs"]], tag="x_park")
    claim(c, "WG0", "WDb0", "WU0", "WDa0")
    p.dma(lambda e: e.dma_start(
        out=sb["WG0"][:], in_=c.ws["wgT0"].ap().rearrange("(a q) r -> q a r", q=128)),
        reads=[c.hb["wgT0"]], writes=[bf(c, "WG0")], tag="wg0_ld")
    p.dma(lambda e: e.dma_start(
        out=sb["WU0"][:], in_=c.ws["wuT0"].ap().rearrange("(a q) r -> q a r", q=128)),
        reads=[c.hb["wuT0"]], writes=[bf(c, "WU0")], tag="wu0_ld")
    p.dma(lambda e: e.dma_start(
        out=sb["WDa0"][:],
        in_=c.ws["wdT0"].ap().rearrange("(a q) r -> q a r", q=128)[:, 0:16, :]),
        reads=[c.hb["wdT0"]], writes=[bf(c, "WDa0")], tag="wda_ld")
    p.dma(lambda e: e.dma_start(
        out=sb["WDb0"][:],
        in_=c.ws["wdT0"].ap().rearrange("(a q) r -> q a r", q=128)[:, 16:22, :]),
        reads=[c.hb["wdT0"]], writes=[bf(c, "WDb0")], tag="wdb_ld")
    ccs = [(0, 1024), (1024, 1024), (2048, 768)]
    for t in range(NT):
        for ci, (c0, cw) in enumerate(ccs):
            for h0 in range(0, cw, 512):
                qsl = slice(h0, min(h0 + 512, cw))
                half = h0 // 512
                for kc in range(NC):
                    p.op("pe", lambda e, t=t, kc=kc, c0=c0, qsl=qsl: e.matmul(
                        c.ps["PSS"][:, qsl], sb["XT"][:, kc, t * 128:(t + 1) * 128],
                        sb["WG0"][:, kc, c0 + qsl.start:c0 + qsl.stop],
                        start=(kc == 0), stop=(kc == NC - 1)),
                        reads=[bf(c, "XT"), bf(c, "WG0")],
                        writes=[c.ps_buf["PSS%d" % half]], tag="g_mm")
                for kc in range(NC):
                    p.op("pe", lambda e, t=t, kc=kc, c0=c0, qsl=qsl: e.matmul(
                        c.ps["PO"][:, qsl], sb["XT"][:, kc, t * 128:(t + 1) * 128],
                        sb["WU0"][:, kc, c0 + qsl.start:c0 + qsl.stop],
                        start=(kc == 0), stop=(kc == NC - 1)),
                        reads=[bf(c, "XT"), bf(c, "WU0")],
                        writes=pob(c, qsl.start, qsl.stop), tag="u_mm")
            pss_r = [c.ps_buf["PSS%d" % q] for q in range((cw + 511) // 512)]
            p.op("act", lambda e, cw=cw: e.activation(sb["SC"][:, 0:cw],
                                                      c.ps["PSS"][:, 0:cw], AF.Silu),
                 reads=pss_r, writes=[bf(c, "SC")], tag="g_silu")
            p.op("dve", lambda e, cw=cw: e.tensor_tensor(
                sb["WB"][:, 0:cw], c.ps["PO"][:, 0:cw], sb["SC"][:, 0:cw], ALU.mult),
                reads=pob(c, 0, cw) + [bf(c, "SC")], writes=[bf(c, "WB")],
                tag="h_mul")
            if DEBUG_SNAP and t == 0 and ci == 0 and seq == 0:
                dbg_snap(c, "h_t0", "WB")
            fp4_quant(c, "WB", sb["WB"][:, 0:cw], sb["QOUT"][:, 0:cw],
                      bf(c, "QOUT"), cw)
            if DEBUG_SNAP and t == 0 and ci == 0 and seq == 0:
                dbg_snap(c, "hq_t0", "QOUT")
            _transpose_1024(c, sb["QOUT"][:], bf(c, "QOUT"), cw // 128)
            p.op("act", lambda e, cw=cw: e.activation(
                sb["HQT"][:].rearrange("p a b -> p (a b)")[:, 0:cw],
                c.ps["PX"][:, 0:cw], AF.Copy),
                reads=[c.ps_buf["PX"]], writes=[bf(c, "HQT")], tag="hq_ev")
            for j in range(cw // 128):
                kcg = c0 // 128 + j
                wt = "WDa0" if kcg < 16 else "WDb0"
                wj = kcg if kcg < 16 else kcg - 16
                for nh in range(2):
                    p.op("pe", lambda e, j=j, wt=wt, wj=wj, nh=nh, ci=ci, t=t: e.matmul(
                        c.ps["PSS"][:, 1024 + nh * 512:1024 + nh * 512 + 512],
                        sb["HQT"][:, j, :], sb[wt][:, wj, nh * 512:(nh + 1) * 512],
                        start=(ci == 0 and j == 0),
                        stop=(ci == 2 and j == cw // 128 - 1)),
                        reads=[bf(c, "HQT"), bf(c, wt)],
                        writes=[c.ps_buf["PSS%d" % (2 + nh)]], tag="d_mm")
        p.dma(lambda e, t=t: e.dma_start(
            out=sb["SC"][:], in_=c.xs.ap()[t * 128:(t + 1) * 128, :]),
            reads=[c.hb["xs"], bf(c, "SC")], writes=[bf(c, "SC")], tag="xr_ld")
        p.op("dve", lambda e, t=t: e.tensor_tensor(
            sb["SC2"][:], c.ps["PSS"][:, 1024:2048], sb["SC"][:], ALU.add),
            reads=[c.ps_buf["PSS2"], c.ps_buf["PSS3"], bf(c, "SC")],
            writes=[bf(c, "SC2")], tag="d_res")
        if DEBUG_SNAP and t == 0 and seq == 0:
            dbg_snap(c, "p1out_t0", "SC2")
        p.dma(lambda e, t=t: e.dma_start(
            out=c.xs.ap()[t * 128:(t + 1) * 128, :], in_=sb["SC2"][:]),
            reads=[bf(c, "SC2")], writes=[c.hb["xs"]], tag="xn_st")
    return "xs"


def emit_p2(c, seq, x_loc, is_last):
    p, sb = c.p, c.sb
    claim(c, "X", "OW", "QP", "KP", "VP", "WS1", "WS2", "QB1", "KB1", "PT")
    if x_loc != "X":
        emit_load_x(c, seq, x_loc)
    emit_ln_to_xt(c)
    p.dma(lambda e: e.dma_start(
        out=sb["OW"][:], in_=c.ws["Wob"].ap().rearrange("(a q) r -> q a r", q=128)),
        reads=[c.hb["Wob"], bf(c, "OW")], writes=[bf(c, "OW")], tag="wo_ld")
    p.dma(lambda e: e.dma_start(out=sb["RM"][0:64, :], in_=c.ws["RMb"].ap()),
          reads=[c.hb["RMb"], bf(c, "RM")], writes=[bf(c, "RM")], tag="rm_ld")
    p.dma(lambda e: e.dma_start(out=sb["RM"][64:128, :], in_=c.ws["RMb"].ap()),
          reads=[c.hb["RMb"], bf(c, "RM")], writes=[bf(c, "RM")], tag="rm_ld2")
    p.dma(lambda e: e.dma_start(out=sb["WB"][0:1, :],
                                in_=c.w_in["L1_attn_bo"].ap().rearrange("d -> () d")),
          reads=[bf(c, "WB")], writes=[bf(c, "WB")], tag="bo_ld")
    p.op("dve", lambda e: e.tensor_copy(sb["ROWB"][0:1, :], sb["WB"][0:1, :]),
         reads=[bf(c, "WB")], writes=[bf(c, "ROWB")], tag="bo_c", drain=True)
    for pair in range(NPAIR):
        psl = slice(pair * 128, (pair + 1) * 128)
        for j, bn in enumerate(("L1_attn_bq", "L1_attn_bk", "L1_attn_bv")):
            p.dma(lambda e, bn=bn, j=j, psl=psl: e.dma_start(
                out=sb["BCOL"][:, j:j + 1],
                in_=c.w_in[bn].ap().rearrange("(a p) -> p a", p=128)[:, pair:pair + 1]
                if False else
                c.w_in[bn].ap()[psl].rearrange("p -> p ()")),
                reads=[bf(c, "BCOL")], writes=[bf(c, "BCOL")], tag="bc_ld")
        if P2_TRUNC == 10:
            continue
        _load_wslice(c, "Wqb", pair, "WS1")
        _proj_fm(c, "WS1", "QP", bf(c, "QP"), "act",
                 bias_col=None if P2_TRUNC == 11 else sb["BCOL"][:, 0:1],
                 extra_evac_reads=[bf(c, "BCOL")])
        if P2_TRUNC == 12:
            continue
        _load_wslice(c, "Wkb", pair, "WS2")
        _proj_fm(c, "WS2", "KP", bf(c, "KP"), "act",
                 bias_col=None if P2_TRUNC == 11 else sb["BCOL"][:, 1:2],
                 extra_evac_reads=[bf(c, "BCOL")])
        if P2_TRUNC == 13:
            continue
        _load_wslice(c, "Wvb", pair, "WS1")
        _proj_v_tokmajor(c, "WS1", pair, bias_col=None)
        if P2_TRUNC == 1:
            continue
        for half in range(2):
            emit_head_performer(c, pair, half)
    if P2_TRUNC in (1, 2, 3, 10, 11, 12, 13, 14, 15):
        return "X"
    p.dma(lambda e: e.dma_start(
        out=sb["BV8"][:], in_=c.w_in["L1_attn_bv"].ap().rearrange(
            "(a p) -> p a", p=128), allow_slow_non_contiguous=True),
        reads=[bf(c, "BV8")], writes=[bf(c, "BV8")], tag="bv_ld")
    p.op("dve", lambda e: e.tensor_copy(sb["BV8B"][:], sb["BV8"][:]),
         reads=[bf(c, "BV8")], writes=[bf(c, "BV8B")], tag="bv_c", drain=True)
    for nh in range(2):
        for kc in range(NC):
            p.op("pe", lambda e, nh=nh, kc=kc: e.matmul(
                c.ps["PO"][0:1, nh * 512:(nh + 1) * 512],
                sb["BV8B"][:, kc:kc + 1], sb["OW"][:, kc, nh * 512:(nh + 1) * 512],
                start=(kc == 0), stop=(kc == NC - 1)),
                reads=[bf(c, "BV8B"), bf(c, "OW")],
                writes=pob(c, nh * 512, (nh + 1) * 512), tag="bv_mm")
    p.op("dve", lambda e: e.tensor_tensor(sb["ROWB"][0:1, :], c.ps["PO"][0:1, 0:1024],
                                          sb["WB"][0:1, :], ALU.add),
         reads=pob(c, 0, 1024) + [bf(c, "WB"), bf(c, "ROWB")],
         writes=[bf(c, "ROWB")], tag="bv_add", drain=True)
    emit_out_proj(c, target="X", bias_row="ROWB")
    return "X"


def emit_head_performer(c, pair, half):
    p, sb = c.p, c.sb
    h = 2 * pair + half
    hsl = slice(64 * half, 64 * half + 64)
    for mf in range(2):
        for scu in range(4):
            ssl = slice(scu * 512, (scu + 1) * 512)
            quarter = scu % 2
            p.op("pe", lambda e, mf=mf, ssl=ssl, quarter=quarter: e.matmul(
                c.ps["PSS"][:, quarter * 512:(quarter + 1) * 512],
                sb["RM"][hsl, mf * 128:(mf + 1) * 128], sb["QP"][hsl, ssl],
                start=True, stop=True),
                reads=[bf(c, "RM"), bf(c, "QP")],
                writes=[c.ps_buf["PSS%d" % quarter]], tag="pf_qmm")
            emit_elu_plus1(c, quarter,
                           sb["QB1"][:, mf, ssl], bf(c, "QB1"))
    for g in range(4):
        for j in range(4):
            t = g * 4 + j
            p.op("pe", lambda e, t=t, j=j: e.matmul(
                c.ps["PSS"][:, 1024 + j * 256:1024 + (j + 1) * 256],
                sb["KP"][hsl, t * 128:(t + 1) * 128], sb["RM"][hsl, :],
                start=True, stop=True),
                reads=[bf(c, "RM"), bf(c, "KP")],
                writes=[c.ps_buf["PSS%d" % (2 + j // 2)]], tag="pf_kmm")
        emit_elu_plus1_wide(c, (2, 3),
                            sb["KB1"][:, g * 4:(g + 1) * 4, :]
                            .rearrange("p t m -> p (t m)"), bf(c, "KB1"))
    for mf in range(2):
        for t in range(NT):
            p.op("pe", lambda e, mf=mf, t=t, half=half: e.matmul(
                c.ps["PO"][:, mf * 65:mf * 65 + 65],
                sb["KB1"][:, t, mf * 128:(mf + 1) * 128], sb["VP"][:, half, t, :],
                start=(t == 0), stop=(t == NT - 1)),
                reads=[bf(c, "KB1"), bf(c, "VP")], writes=pob(c, 0, 130),
                tag="pf_kv")
    p.op("dve", lambda e: e.tensor_copy(
        sb["KVS"][:].rearrange("p a b -> p (a b)"), c.ps["PO"][:, 0:130]),
        reads=pob(c, 0, 130), writes=[bf(c, "KVS")], tag="pf_kve", drain=True)
    if P2_TRUNC == 2:
        return
    for t in range(NT):
        for mf in range(2):
            p.op("pe", lambda e, t=t, mf=mf: e.matmul(
                c.ps["PO"][:, po_off(t):po_off(t) + 65],
                sb["QB1"][:, mf, t * 128:(t + 1) * 128], sb["KVS"][:, mf, :],
                start=(mf == 0), stop=(mf == 1)),
                reads=[bf(c, "QB1"), bf(c, "KVS")],
                writes=pob(c, po_off(t), po_off(t) + 65), tag="pf_out")
    emit_attn_norm_evac(c, h, clamp=2.56e-4)


def emit_elu_plus1(c, quarter, dst_ap, dst_buf):
    p, sb = c.p, c.sb
    qsl = slice(quarter * 512, (quarter + 1) * 512)
    pb = c.ps_buf["PSS%d" % quarter]
    p.op("dve", lambda e: e.tensor_scalar_min(sb["SC"][:, 0:512],
                                              c.ps["PSS"][:, qsl], 0.0),
         reads=[pb], writes=[bf(c, "SC")], tag="el_min")
    p.op("act", lambda e: e.activation(sb["SC"][:, 0:512], sb["SC"][:, 0:512],
                                       AF.Exp),
         reads=[bf(c, "SC")], writes=[bf(c, "SC")], tag="el_exp")
    p.op("dve", lambda e: e.scalar_tensor_tensor(dst_ap, c.ps["PSS"][:, qsl], 0.0,
                                                 sb["SC"][:, 0:512], ALU.max,
                                                 ALU.add),
         reads=[pb, bf(c, "SC")], writes=[dst_buf], tag="el_stt")


def emit_elu_plus1_wide(c, quarters, dst_ap, dst_buf):
    p, sb = c.p, c.sb
    pbs = [c.ps_buf["PSS%d" % q] for q in quarters]
    p.op("dve", lambda e: e.tensor_scalar_min(sb["SC"][:, 0:1024],
                                              c.ps["PSS"][:, 1024:2048], 0.0),
         reads=pbs, writes=[bf(c, "SC")], tag="el_min")
    p.op("act", lambda e: e.activation(sb["SC"][:, 0:1024], sb["SC"][:, 0:1024],
                                       AF.Exp),
         reads=[bf(c, "SC")], writes=[bf(c, "SC")], tag="el_exp")
    p.op("dve", lambda e: e.scalar_tensor_tensor(dst_ap, c.ps["PSS"][:, 1024:2048],
                                                 0.0, sb["SC"][:, 0:1024], ALU.max,
                                                 ALU.add),
         reads=pbs + [bf(c, "SC")], writes=[dst_buf], tag="el_stt")


def emit_p3(c, seq, x_loc, is_last):
    p, sb = c.p, c.sb
    claim(c, "X")
    if x_loc != "X":
        emit_load_x(c, seq, x_loc)
    emit_ln_stats(c, lambda t: sb["X"][:, t, :], lambda t: [bf(c, "X")])
    for t in range(NT):
        emit_ln_apply(c, t, sb["X"][:, t, :], sb["SC"][:],
                      reads=[bf(c, "X")], writes=[bf(c, "SC")])
        emit_act_quant(c, "SC", sb["SC"][:], 1024, sb["QOUT"][:], bf(c, "QOUT"),
                       sm_col=t)
        if DEBUG_SNAP and t == 0 and seq == 0:
            dbg_snap(c, "p3xq_t0", "QOUT")
            dbg_snap(c, "p3sc_t0", "SC")
            dbg_snap(c, "p3sm_t0", "SM", ap=sb["SM"][:, 32:35])
        emit_xt_from_bf16_tiles(c, t, sb["QOUT"][:], bf(c, "QOUT"))
    p.dma(lambda e: e.dma_start(
        out=c.xs.ap().rearrange("(t p) d -> p t d", p=128), in_=sb["X"][:]),
        reads=[bf(c, "X")], writes=[c.hb["xs"]], tag="x_park3")
    for hf in range(2):
        claim(c, "H1", "WG1c", "WU1c")
        p.op("dve", lambda e: e.memset(sb["SM2"][:, 0:8], 0.0),
             writes=[bf(c, "SM2")], tag="p3_z")
        for ci in range(4):
            for wname, wsrc in (("WG1c", "wgT1"), ("WU1c", "wuT1")):
                p.dma(lambda e, wname=wname, wsrc=wsrc, ci=ci: e.dma_start(
                    out=sb[wname][:],
                    in_=c.ws[wsrc].ap().rearrange("(a q) r -> q a r", q=128)
                        [:, :, ci * 1024:(ci + 1) * 1024]),
                    reads=[c.hb[wsrc], bf(c, wname)], writes=[bf(c, wname)],
                    tag="p3_wld")
            for tj in range(8):
                t = hf * 8 + tj
                for half in range(2):
                    qsl = slice(half * 512, half * 512 + 512)
                    for kc in range(NC):
                        p.op("pe", lambda e, t=t, kc=kc, qsl=qsl: e.matmul(
                            c.ps["PSS"][:, qsl],
                            sb["XT"][:, kc, t * 128:(t + 1) * 128],
                            sb["WG1c"][:, kc, qsl], start=(kc == 0),
                            stop=(kc == NC - 1)),
                            reads=[bf(c, "XT"), bf(c, "WG1c")],
                            writes=[c.ps_buf["PSS%d" % half]], tag="p3_gmm")
                    for kc in range(NC):
                        p.op("pe", lambda e, t=t, kc=kc, qsl=qsl: e.matmul(
                            c.ps["PO"][:, qsl],
                            sb["XT"][:, kc, t * 128:(t + 1) * 128],
                            sb["WU1c"][:, kc, qsl], start=(kc == 0),
                            stop=(kc == NC - 1)),
                            reads=[bf(c, "XT"), bf(c, "WU1c")],
                            writes=pob(c, qsl.start, qsl.stop), tag="p3_umm")
                p.op("dve", lambda e, t=t: e.tensor_tensor(
                    sb["SM2"][:, 8:9], sb["SM"][:, t:t + 1], sb["CN"][:, 1:2],
                    ALU.mult),
                    reads=[bf(c, "SM"), bf(c, "CN")], writes=[bf(c, "SM2")],
                    tag="p3_dqg")
                p.op("dve", lambda e, t=t: e.tensor_tensor(
                    sb["SM2"][:, 9:10], sb["SM"][:, t:t + 1], sb["CN"][:, 2:3],
                    ALU.mult),
                    reads=[bf(c, "SM"), bf(c, "CN")], writes=[bf(c, "SM2")],
                    tag="p3_dqu", drain=True)
                p.op("act", lambda e: e.activation(sb["SC"][:, 0:1024],
                                                   c.ps["PSS"][:, 0:1024], AF.Silu,
                                                   scale=sb["SM2"][:, 8:9]),
                     reads=[c.ps_buf["PSS0"], c.ps_buf["PSS1"], bf(c, "SM2")],
                     writes=[bf(c, "SC")], tag="p3_silu")
                p.op("dve", lambda e, tj=tj, ci=ci: e.scalar_tensor_tensor(
                    sb["H1"][:, tj, ci * 1024:(ci + 1) * 1024],
                    c.ps["PO"][:, 0:1024], sb["SM2"][:, 9:10], sb["SC"][:, 0:1024],
                    ALU.mult, ALU.mult),
                    reads=pob(c, 0, 1024) + [bf(c, "SM2"), bf(c, "SC")],
                    writes=[bf(c, "H1")], tag="p3_h")
                if DEBUG_SNAP and tj == 0 and ci == 0 and hf == 0 and seq == 0:
                    dbg_snap(c, "p3h_t0", "H1", ap=sb["H1"][:, 0, 0:1024])
                p.op("dve", lambda e, tj=tj, ci=ci: e.tensor_reduce(
                    sb["SM2"][:, 16 + tj:17 + tj],
                    sb["H1"][:, tj, ci * 1024:(ci + 1) * 1024], AX.X, ALU.max,
                    apply_absolute_value=True),
                    reads=[bf(c, "H1")], writes=[bf(c, "SM2")], tag="p3_hm")
                p.op("dve", lambda e, tj=tj: e.tensor_tensor(
                    sb["SM2"][:, tj:tj + 1], sb["SM2"][:, tj:tj + 1],
                    sb["SM2"][:, 16 + tj:17 + tj], ALU.max),
                    reads=[bf(c, "SM2")], writes=[bf(c, "SM2")], tag="p3_hmu")
        claim(c, "WD1a", "WD1b")
        p.dma(lambda e: e.dma_start(
            out=sb["WD1a"][:],
            in_=c.ws["wdT1"].ap().rearrange("(a q) r -> q a r", q=128)[:, 0:16, :]),
            reads=[c.hb["wdT1"], bf(c, "WD1a")], writes=[bf(c, "WD1a")],
            tag="p3_wda")
        p.dma(lambda e: e.dma_start(
            out=sb["WD1b"][:],
            in_=c.ws["wdT1"].ap().rearrange("(a q) r -> q a r", q=128)[:, 16:32, :]),
            reads=[c.hb["wdT1"], bf(c, "WD1b")], writes=[bf(c, "WD1b")],
            tag="p3_wdb")
        for tj in range(8):
            t = hf * 8 + tj
            p.op("dve", lambda e, tj=tj: e.tensor_scalar_max(
                sb["SM2"][:, 24:25], sb["SM2"][:, tj:tj + 1], 1e-5),
                reads=[bf(c, "SM2")], writes=[bf(c, "SM2")], tag="p3_sc")
            p.op("dve", lambda e: e.reciprocal(sb["SM2"][:, 25:26],
                                               sb["SM2"][:, 24:25]),
                 reads=[bf(c, "SM2")], writes=[bf(c, "SM2")], tag="p3_sr")
            p.op("dve", lambda e: e.tensor_scalar_mul(sb["SM2"][:, 26:27],
                                                      sb["SM2"][:, 25:26], 127.0),
                 reads=[bf(c, "SM2")], writes=[bf(c, "SM2")], tag="p3_s")
            p.op("dve", lambda e: e.tensor_scalar_mul(sb["SM2"][:, 27:28],
                                                      sb["SM2"][:, 24:25],
                                                      1.0 / 127.0),
                 reads=[bf(c, "SM2")], writes=[bf(c, "SM2")], tag="p3_i")
            p.op("dve", lambda e: e.tensor_tensor(sb["SM2"][:, 27:28],
                                                  sb["SM2"][:, 27:28],
                                                  sb["CN"][:, 3:4], ALU.mult),
                 reads=[bf(c, "SM2"), bf(c, "CN")], writes=[bf(c, "SM2")],
                 tag="p3_dq2")
            for ci in range(4):
                p.op("dve", lambda e, tj=tj, ci=ci: e.tensor_single_scalar(
                    sb["SC2"][:, 0:1024], sb["H1"][:, tj, ci * 1024:(ci + 1) * 1024],
                    sb["SM2"][:, 26:27], ALU.mult),
                    reads=[bf(c, "H1"), bf(c, "SM2")], writes=[bf(c, "SC2")],
                    tag="p3_qm0")
                p.op("dve", lambda e: e.tensor_scalar(
                    sb["SC2"][:, 0:1024], sb["SC2"][:, 0:1024],
                    -128.0, 127.0, ALU.max, ALU.min),
                    reads=[bf(c, "SC2")], writes=[bf(c, "SC2")], tag="p3_qcf")
                p.op("dve", lambda e: e.tensor_scalar_add(
                    sb["SC2"][:, 0:1024], sb["SC2"][:, 0:1024], MAGIC),
                    reads=[bf(c, "SC2")], writes=[bf(c, "SC2")], tag="p3_qm")
                p.op("dve", lambda e: e.tensor_single_scalar(
                    sb["QI"][:, 0:1024], sb["SC2"][:, 0:1024].bitcast(I32),
                    0, ALU.bitwise_or),
                    reads=[bf(c, "SC2")], writes=[bf(c, "QI")], tag="p3_qc")
                p.op("dve", lambda e: e.tensor_scalar_add(
                    sb["QOUT"][:, 0:1024], sb["QI"][:, 0:1024].bitcast(F32), -MAGIC),
                    reads=[bf(c, "QI")], writes=[bf(c, "QOUT")], tag="p3_qb")
                _transpose_1024(c, sb["QOUT"][:], bf(c, "QOUT"), 8)
                p.op("dve", lambda e: e.tensor_copy(
                    sb["HQT"][:].rearrange("p a b -> p (a b)"), c.ps["PX"][:]),
                    reads=[c.ps_buf["PX"]], writes=[bf(c, "HQT")], tag="p3_he")
                for j in range(8):
                    kcg = ci * 8 + j
                    wt = "WD1a" if kcg < 16 else "WD1b"
                    wj = kcg if kcg < 16 else kcg - 16
                    for nh in range(2):
                        p.op("pe", lambda e, j=j, wt=wt, wj=wj, nh=nh, ci=ci: e.matmul(
                            c.ps["PSS"][:, 1024 + nh * 512:1024 + nh * 512 + 512],
                            sb["HQT"][:, j, :],
                            sb[wt][:, wj, nh * 512:(nh + 1) * 512],
                            start=(ci == 0 and j == 0), stop=(ci == 3 and j == 7)),
                            reads=[bf(c, "HQT"), bf(c, wt)],
                            writes=[c.ps_buf["PSS%d" % (2 + nh)]], tag="p3_dmm")
            p.dma(lambda e, t=t: e.dma_start(
                out=sb["SC"][:], in_=c.xs.ap()[t * 128:(t + 1) * 128, :]),
                reads=[c.hb["xs"], bf(c, "SC")], writes=[bf(c, "SC")], tag="p3_xl")
            p.op("dve", lambda e, t=t: e.scalar_tensor_tensor(
                sb["SC2"][:], c.ps["PSS"][:, 1024:2048], sb["SM2"][:, 27:28],
                sb["SC"][:], ALU.mult, ALU.add),
                reads=[c.ps_buf["PSS2"], c.ps_buf["PSS3"], bf(c, "SM2"),
                       bf(c, "SC")],
                writes=[bf(c, "SC2")], tag="p3_res")
            if DEBUG_SNAP and t == 0 and seq == 0:
                dbg_snap(c, "p3out_t0", "SC2")
            if is_last:
                p.dma(lambda e, t=t, seq=seq: e.dma_start(
                    out=c.out.ap()[seq, t * 128:(t + 1) * 128, :], in_=sb["SC2"][:]),
                    reads=[bf(c, "SC2")], writes=[c.hb["out"]], tag="p3_out")
            else:
                p.dma(lambda e, t=t: e.dma_start(
                    out=c.xs.ap()[t * 128:(t + 1) * 128, :], in_=sb["SC2"][:]),
                    reads=[bf(c, "SC2")], writes=[c.hb["xs"]], tag="p3_xs")
    return "out" if is_last else "xs"


def emit_act_quant(c, src_name, src_ap, n, dst_ap, dst_buf, sm_col):
    p, sb = c.p, c.sb
    srcb = bf(c, src_name)
    p.op("dve", lambda e: e.tensor_reduce(sb["SM"][:, 32:33], src_ap, AX.X, ALU.max,
                                          apply_absolute_value=True),
         reads=[srcb], writes=[bf(c, "SM")], tag="aq_max")
    p.op("dve", lambda e: e.tensor_scalar_max(sb["SM"][:, 32:33], sb["SM"][:, 32:33],
                                              1e-5),
         reads=[bf(c, "SM")], writes=[bf(c, "SM")], tag="aq_cl")
    p.op("dve", lambda e, sm_col=sm_col: e.tensor_scalar_mul(
        sb["SM"][:, sm_col:sm_col + 1], sb["SM"][:, 32:33], 1.0 / 127.0),
        reads=[bf(c, "SM")], writes=[bf(c, "SM")], tag="aq_inv")
    p.op("dve", lambda e: e.reciprocal(sb["SM"][:, 33:34], sb["SM"][:, 32:33]),
         reads=[bf(c, "SM")], writes=[bf(c, "SM")], tag="aq_rcp")
    p.op("dve", lambda e: e.tensor_scalar_mul(sb["SM"][:, 34:35], sb["SM"][:, 33:34],
                                              127.0),
         reads=[bf(c, "SM")], writes=[bf(c, "SM")], tag="aq_s")
    p.op("dve", lambda e, n=n: e.tensor_single_scalar(sb["SC2"][:, 0:n], src_ap,
                                                      sb["SM"][:, 34:35], ALU.mult),
         reads=[srcb, bf(c, "SM")], writes=[bf(c, "SC2")], tag="aq_m0")
    p.op("dve", lambda e, n=n: e.tensor_scalar(sb["SC2"][:, 0:n],
                                               sb["SC2"][:, 0:n],
                                               -128.0, 127.0, ALU.max, ALU.min),
         reads=[bf(c, "SC2")], writes=[bf(c, "SC2")], tag="aq_clf")
    p.op("dve", lambda e, n=n: e.tensor_scalar_add(sb["SC2"][:, 0:n],
                                                   sb["SC2"][:, 0:n], MAGIC),
         reads=[bf(c, "SC2")], writes=[bf(c, "SC2")], tag="aq_m")
    if DEBUG_SNAP and sm_col == 0:
        dbg_snap(c, "aqm_sc2", "SC2")
    p.op("dve", lambda e, n=n: e.tensor_single_scalar(sb["QI"][:, 0:n],
                                                      sb["SC2"][:, 0:n].bitcast(I32),
                                                      0, ALU.bitwise_or),
         reads=[bf(c, "SC2")], writes=[bf(c, "QI")], tag="aq_c")
    p.op("dve", lambda e, n=n: e.tensor_scalar_add(dst_ap,
                                                   sb["QI"][:, 0:n].bitcast(F32),
                                                   -MAGIC),
         reads=[bf(c, "QI")], writes=[dst_buf], tag="aq_b")


def emit_epilogue(c, seq, x_loc):
    p, sb = c.p, c.sb
    toks = []
    if x_loc == "out":
        pass
    elif x_loc == "X":
        toks.append(p.dma(lambda e, seq=seq: e.dma_start(
            out=c.out.ap()[seq].rearrange("(t p) d -> p t d", p=128), in_=sb["X"][:]),
            reads=[bf(c, "X")], writes=[c.hb["out"]], tag="ep_x"))
    elif x_loc == "xs":
        toks.append(p.dma(lambda e, seq=seq: e.dma_start(
            out=c.out.ap()[seq], in_=c.xs.ap()),
            reads=[c.hb["xs"]], writes=[c.hb["out"]], tag="ep_xs"))
    else:
        toks.append(p.dma(lambda e, seq=seq: e.dma_start(
            out=c.out.ap()[seq], in_=c.x_in.ap()[seq]),
            reads=[c.hb["x"]], writes=[c.hb["out"]], tag="ep_in"))
    if x_loc == "out":
        for rec in c.p.stream["qs"]:
            if rec["tag"] == "p3_out":
                toks.append(rec["tok"])
    return toks


PHASE_FNS = {0: emit_p0, 1: emit_p1, 2: emit_p2, 3: emit_p3}

_NC_CACHE = {}


def get_program(phases=(0, 1, 2, 3)):
    key = tuple(phases)
    if key not in _NC_CACHE:
        _NC_CACHE[key] = build_program(phases)
    return _NC_CACHE[key]


def kernel(**inputs):
    x = np.ascontiguousarray(inputs["L0_attn_x"], dtype=np.float32)
    nc = get_program()
    in_maps = []
    for core in range(NCORES):
        m = {"x": x[SEQ_PER_CORE * core:SEQ_PER_CORE * (core + 1)]}
        for n in WEIGHT_NAMES:
            m[n] = np.ascontiguousarray(inputs[n], dtype=np.float32)
        in_maps.append(m)
    res = run_bass_kernel_spmd(nc, in_maps, core_ids=list(range(NCORES)))
    out = np.concatenate([np.asarray(r["out"], dtype=np.float32)
                          for r in res.results], axis=0)
    return out.reshape(B, S, D)


# version 07
# speedup vs baseline: 1.0089x; 1.0089x over previous
import numpy as np
import concourse.bass as bass
import concourse.mybir as mybir
from concourse.bass_utils import run_bass_kernel_spmd

F32 = mybir.dt.float32
BF16 = mybir.dt.bfloat16
I32 = mybir.dt.int32
AF = mybir.ActivationFunctionType
ALU = mybir.AluOpType
AX = mybir.AxisListType

B, S, D = 16, 2048, 1024
NCORES = 8
SEQ_PER_CORE = B // NCORES
NT = S // 128
NC = D // 128
H = 16
HD = 64
NPAIR = 8
I0 = 2816
I1 = 4096
M1 = 256
EPS_LN = 1e-5
MAGIC = 12582912.0
MAGIC_I = 0x4B400000

WEIGHT_NAMES = [
    "L0_attn_q_w", "L0_attn_k_w", "L0_attn_v_w", "L0_attn_o_w",
    "L0_ffn_w_gate", "L0_ffn_w_up", "L0_ffn_w_down",
    "L1_attn_Wq", "L1_attn_bq", "L1_attn_Wk", "L1_attn_bk",
    "L1_attn_Wv", "L1_attn_bv", "L1_attn_Wo", "L1_attn_bo",
    "L1_attn_random_matrix",
    "L1_ffn_w_gate", "L1_ffn_w_up", "L1_ffn_w_down",
]
SLOPES = [2.0 ** (-0.5 * (h + 1)) for h in range(H)]


def po_off(it):
    return (it // 7) * 512 + (it % 7) * 65


class Buf:
    __slots__ = ("name", "last_w", "readers")

    def __init__(self, name):
        self.name = name
        self.last_w = None
        self.readers = []


class Prog:
    COMPUTE = ("pe", "dve", "act", "pool")
    QUEUES = ("qs",)

    def __init__(self):
        self.stream = {e: [] for e in self.COMPUTE + self.QUEUES}
        self.order = []

    def _deps(self, reads, writes):
        deps = set()
        for b in reads:
            if b.last_w is not None:
                deps.add(b.last_w)
        for b in writes:
            if b.last_w is not None:
                deps.add(b.last_w)
            deps.update(b.readers)
        return deps

    def _commit(self, eng, rec, reads, writes):
        tok = (eng, len(self.stream[eng]))
        rec["tok"] = tok
        self.stream[eng].append(rec)
        self.order.append(tok)
        for b in reads:
            b.readers.append(tok)
        for b in writes:
            b.last_w = tok
            b.readers = []
        return tok

    def op(self, eng, emit, reads=(), writes=(), tag="", drain=False):
        assert eng in self.COMPUTE
        deps = self._deps(reads, writes)
        return self._commit(eng, dict(emit=emit, deps=deps, dma=False, tag=tag,
                                      drain=drain), reads, writes)

    def dma(self, emit, reads=(), writes=(), tag=""):
        deps = self._deps(reads, writes)
        return self._commit("qs", dict(emit=emit, deps=deps, dma=True, tag=tag),
                            reads, writes)

    def finalize(self, nc, final_tokens):
        known = {e: {} for e in self.stream}
        own_sync = {e: -1 for e in self.stream}
        ksnap = {}
        inc_needed = {e: set() for e in self.stream}
        for tok in self.order:
            eng, idx = tok
            rec = self.stream[eng][idx]
            k = known[eng]
            waits = []
            for (fe, fj) in sorted(rec["deps"]):
                if fe == eng and fe not in self.QUEUES:
                    if eng == "pe":
                        continue
                    if idx - fj > 4 or own_sync[eng] >= fj:
                        continue
                    own_sync[eng] = fj
                    waits.append((fe, fj))
                    inc_needed[fe].add(fj)
                    continue
                if k.get(fe, -1) >= fj:
                    continue
                waits.append((fe, fj))
                inc_needed[fe].add(fj)
                for ge, gj in ksnap[(fe, fj)].items():
                    if k.get(ge, -1) < gj:
                        k[ge] = gj
                k[fe] = max(k.get(fe, -1), fj)
            rec["waits"] = waits
            if eng not in self.QUEUES:
                k[eng] = idx
                ksnap[tok] = dict(k)
            else:
                snap = dict(k)
                snap[eng] = idx
                ksnap[tok] = snap
        for (fe, fj) in final_tokens:
            inc_needed[fe].add(fj)
        semval = {}
        for e, ops in self.stream.items():
            v = 0
            for j, rec in enumerate(ops):
                rec["inc"] = (e in self.QUEUES) or (j in inc_needed[e])
                if rec["inc"]:
                    v += 16 if e in self.QUEUES else 1
                rec["semval"] = v
            semval[e] = v
        sems = {e: nc.alloc_semaphore("tl_" + e) for e in self.stream}

        def val_of(fe, fj):
            return self.stream[fe][fj]["semval"]

        def body(engobj, e):
            for rec in self.stream[e]:
                for (fe, fj) in rec["waits"]:
                    engobj.wait_ge(sems[fe], val_of(fe, fj))
                inst = rec["emit"](engobj)
                if rec["inc"]:
                    inst.then_inc(sems[e], 16 if rec["dma"] else 1)

        with nc.Block() as block:
            @block.tensor
            def _(t):
                body(t, "pe")

            @block.vector
            def _(v):
                body(v, "dve")

            @block.scalar
            def _(s):
                body(s, "act")

            @block.gpsimd
            def _(g):
                body(g, "pool")

            @block.sync
            def _(sp):
                body(sp, "qs")
                for (fe, fj) in final_tokens:
                    sp.wait_ge(sems[fe], val_of(fe, fj))


class Ctx:
    pass


SB_BASE = 16512


def _sb_at(c, name, shape, dtype, off):
    off = off + SB_BASE
    t = c.nc.alloc_sbuf_tensor_at(name, shape, dtype, offset=off)
    c.sb[name] = t
    itemsize = {F32: 4, BF16: 2, I32: 4}[dtype]
    per_part = int(np.prod(shape[1:])) * itemsize
    shared = None
    for o, (lo, hi) in c.ranges.items():
        if lo == off and hi == off + per_part:
            shared = c.sbuf_buf[o]
            break
    c.sbuf_buf[name] = shared if shared is not None else Buf(name)
    c.ranges[name] = (off, off + per_part)
    return t


def claim(c, *names):
    for n in names:
        lo, hi = c.ranges[n]
        me = c.sbuf_buf[n]
        for o, (lo2, hi2) in c.ranges.items():
            ob = c.sbuf_buf[o]
            if ob is me or hi2 <= lo or hi <= lo2:
                continue
            if ob.last_w is not None:
                me.readers.append(ob.last_w)
            me.readers.extend(ob.readers)
            ob.last_w = None
            ob.readers = []


def pob(c, c0, c1):
    return [c.ps_buf["PO%d" % b] for b in range(c0 // 512, (c1 + 511) // 512)]


def sbuf(c, n):
    return c.sb[n]


def bf(c, n):
    return c.sbuf_buf[n] if n in c.sbuf_buf else c.ps_buf[n]


DEBUG_DUMPS = []
DEBUG_SNAP = False
_SNAP_N = [0]


def dbg_snap(c, label, name, ap=None):
    if not DEBUG_SNAP:
        return
    t = c.sb[name]
    a = ap if ap is not None else t[:]
    dout = c.nc.dram_tensor("snap%d_%s" % (_SNAP_N[0], label), list(a.shape),
                            a.dtype, kind="ExternalOutput")
    _SNAP_N[0] += 1
    tok = c.p.dma(lambda e: e.dma_start(out=dout.ap(), in_=a),
                  reads=[bf(c, name)], writes=[], tag="snap")
    c.snap_toks.append(tok)


RACE_DETECT = True
P2_TRUNC = 0


def build_program(phases=(0, 1, 2, 3)):
    nc = bass.Bass("TRN2", target_bir_lowering=False,
                   detect_race_conditions=RACE_DETECT)
    c = Ctx()
    c.nc = nc
    c.p = Prog()
    c.sb, c.sbuf_buf, c.ranges = {}, {}, {}
    c.snap_toks = []
    c.ps, c.ps_buf = {}, {}
    c.phases = tuple(phases)

    shapes = {
        "L0_attn_q_w": [D, D], "L0_attn_k_w": [D, D], "L0_attn_v_w": [D, D],
        "L0_attn_o_w": [D, D],
        "L0_ffn_w_gate": [I0, D], "L0_ffn_w_up": [I0, D], "L0_ffn_w_down": [D, I0],
        "L1_attn_Wq": [D, D], "L1_attn_bq": [D], "L1_attn_Wk": [D, D],
        "L1_attn_bk": [D], "L1_attn_Wv": [D, D], "L1_attn_bv": [D],
        "L1_attn_Wo": [D, D], "L1_attn_bo": [D],
        "L1_attn_random_matrix": [HD, M1],
        "L1_ffn_w_gate": [I1, D], "L1_ffn_w_up": [I1, D], "L1_ffn_w_down": [D, I1],
    }
    c.x_in = nc.dram_tensor("x", [SEQ_PER_CORE, S, D], F32, kind="ExternalInput")
    c.w_in = {n: nc.dram_tensor(n, shp, F32, kind="ExternalInput")
              for n, shp in shapes.items()}
    c.out = nc.dram_tensor("out", [SEQ_PER_CORE, S, D], F32, kind="ExternalOutput")
    c.ws = {}
    for n, shp in [("qb", [D, D]), ("kb", [D, D]), ("vb", [D, D]), ("ob", [D, D]),
                   ("Wqb", [D, D]), ("Wkb", [D, D]), ("Wvb", [D, D]), ("Wob", [D, D]),
                   ("RMb", [HD, M1]),
                   ("wgT0", [D, I0]), ("wuT0", [D, I0]), ("wdT0", [I0, D]),
                   ("wgT1", [D, I1]), ("wuT1", [D, I1]), ("wdT1", [I1, D])]:
        c.ws[n] = nc.dram_tensor("ws_" + n, shp, BF16, kind="Internal")
    c.xs = nc.dram_tensor("xs_scratch", [S, D], F32, kind="Internal")
    c.hb = {n: Buf("hbm_" + n) for n in list(c.ws) + ["xs", "out", "x"]}

    _sb_at(c, "X", [128, NT, D], F32, 0)
    _sb_at(c, "WG0", [128, NC, I0], BF16, 0)
    _sb_at(c, "WDb0", [128, 6, D], BF16, 45056)
    _sb_at(c, "H1", [128, 8, I1], BF16, 0)
    _sb_at(c, "XT", [128, NC, S], BF16, 65536)
    _sb_at(c, "BB", [128, NT, D], BF16, 98304)
    _sb_at(c, "WDa0", [128, 16, D], BF16, 98304)
    _sb_at(c, "WD1a", [128, 16, D], BF16, 98304)
    _sb_at(c, "OW", [128, NC, D], BF16, 131072)
    _sb_at(c, "WU0", [128, NC, I0], BF16, 131072)
    _sb_at(c, "WG1c", [128, NC, 1024], BF16, 131072)
    _sb_at(c, "WU1c", [128, NC, 1024], BF16, 147456)
    _sb_at(c, "WD1b", [128, 16, D], BF16, 131072)
    _sb_at(c, "QP", [128, S], BF16, 147456)
    _sb_at(c, "KP", [128, S], BF16, 151552)
    _sb_at(c, "VP", [128, 2, NT, HD + 1], BF16, 155648)
    _sb_at(c, "WS1", [128, NC, 128], BF16, 159872)
    _sb_at(c, "WS2", [128, NC, 128], BF16, 161920)
    _sb_at(c, "QB1", [128, 2, S], BF16, 163968)
    _sb_at(c, "PT2", [128, S], BF16, 163968)
    _sb_at(c, "WB2", [128, 1024], F32, 168064)
    _sb_at(c, "QOUT2", [128, 1024], BF16, 172160)
    _sb_at(c, "HQT2", [128, 8, 128], BF16, 174208)
    _sb_at(c, "KB1", [128, NT, M1], BF16, 172160)
    E = 180352
    off = [E]

    def nxt(n):
        o = off[0]
        off[0] = (o + n + 31) // 32 * 32
        return o

    wb_off = nxt(4096)
    _sb_at(c, "WB", [128, 1024], F32, wb_off)
    _sb_at(c, "ROWSC", [1, S], BF16, wb_off)
    sc_off = nxt(4096)
    _sb_at(c, "SC", [128, 1024], F32, sc_off)
    _sb_at(c, "QH", [67, S], BF16, sc_off)
    sc2_off = nxt(4096)
    _sb_at(c, "SC2", [128, 1024], F32, sc2_off)
    _sb_at(c, "KH", [67, S], BF16, sc2_off)
    qi_off = nxt(4096)
    _sb_at(c, "QI", [128, 1024], I32, qi_off)
    _sb_at(c, "PT", [128, S], BF16, qi_off)
    qi2_off = nxt(4096)
    _sb_at(c, "QI2", [128, 1024], I32, qi2_off)
    _sb_at(c, "NIO", [65, S], BF16, qi2_off)
    _sb_at(c, "QOUT", [128, 1024], BF16, nxt(2048))
    _sb_at(c, "HQT", [128, 8, 128], BF16, nxt(2048))
    _sb_at(c, "ST", [128, 64], F32, nxt(256))
    _sb_at(c, "ST2", [128, 64], F32, nxt(256))
    _sb_at(c, "CN", [128, 8], F32, nxt(32))
    _sb_at(c, "IOTA", [128, 16], F32, nxt(64))
    _sb_at(c, "BIAS", [128, 16], F32, nxt(64))
    _sb_at(c, "ONES2", [128, 2], BF16, nxt(4))
    _sb_at(c, "TRI", [128, 128], BF16, nxt(256))
    _sb_at(c, "NEGI", [128, 128], BF16, nxt(256))
    _sb_at(c, "IDN", [128, 128], BF16, nxt(256))
    _sb_at(c, "NRM", [2, 16], F32, nxt(64))
    _sb_at(c, "MB", [128, 1], F32, nxt(4))
    _sb_at(c, "ONE1", [1, 128], BF16, nxt(256))
    _sb_at(c, "ONEC", [128, 1], BF16, nxt(2))
    _sb_at(c, "BCAST", [1, 8], BF16, nxt(16))
    _sb_at(c, "MCOL", [2, 4], BF16, nxt(8))
    _sb_at(c, "BCOL", [128, 4], F32, nxt(16))
    _sb_at(c, "ROWB", [1, 1024], BF16, nxt(2048))
    _sb_at(c, "KVS", [128, 2, HD + 1], BF16, nxt(260))
    _sb_at(c, "RM", [128, M1], BF16, nxt(512))
    _sb_at(c, "SM", [128, 64], F32, nxt(256))
    _sb_at(c, "SM2", [128, 64], F32, nxt(256))
    _sb_at(c, "BV8", [128, 8], F32, nxt(32))
    _sb_at(c, "BV8B", [128, 8], BF16, nxt(16))
    assert off[0] + SB_BASE < 229344, off[0]

    for name, shape, dt in [("PSS", [128, 2048], F32), ("PO", [128, 1536], F32),
                            ("PX", [128, 1024], BF16)]:
        c.ps[name] = nc.alloc_psum_tensor(name, shape, dt)
        c.ps_buf[name] = Buf("ps_" + name)
    for q in range(4):
        c.ps_buf["PSS%d" % q] = Buf("ps_PSS%d" % q)
    for q in range(3):
        c.ps_buf["PO%d" % q] = Buf("ps_PO%d" % q)

    emit_consts(c)
    emit_weight_prep(c)
    final = []
    nseq = SEQ_PER_CORE
    for seq in range(nseq):
        x_loc = "hbm_in"
        for ph in c.phases:
            x_loc = PHASE_FNS[ph](c, seq, x_loc, is_last=(ph == c.phases[-1]))
        final += emit_epilogue(c, seq, x_loc)
    for name in DEBUG_DUMPS:
        if name in c.ws:
            t = c.ws[name]
            dout = nc.dram_tensor("dbg_" + name, list(t.shape), t.dtype,
                                  kind="ExternalOutput")
            tok = c.p.dma(lambda e, t=t, dout=dout: e.dma_start(out=dout.ap(),
                                                                in_=t.ap()),
                          reads=[c.hb[name]], writes=[], tag="dbg")
            final.append(tok)
            continue
        t = c.sb[name]
        dshape = list(t.shape)
        dt = t.dtype
        dout = nc.dram_tensor("dbg_" + name, dshape, dt, kind="ExternalOutput")
        tok = c.p.dma(lambda e, t=t, dout=dout: e.dma_start(out=dout.ap(),
                                                            in_=t[:]),
                      reads=[bf(c, name)], writes=[], tag="dbg")
        final.append(tok)
    final += c.snap_toks
    c.p.finalize(nc, final)
    return nc


def emit_consts(c):
    nc, p = c.nc, c.p
    idn_np = np.eye(128, dtype=np.float32)
    tri_np = np.triu(np.ones((128, 128), dtype=np.float32), 1)
    c.idn_dram = nc.inline_tensor(idn_np, name="idn_c")
    c.tri_dram = nc.inline_tensor(tri_np, name="tri_c")
    sb = c.sb

    p.dma(lambda e: e.dma_start(out=sb["SC"][:, 0:128], in_=c.idn_dram.ap()),
          writes=[bf(c, "SC")], tag="c_idn")
    p.op("dve", lambda e: e.tensor_copy(sb["IDN"][:], sb["SC"][:, 0:128]),
         reads=[bf(c, "SC")], writes=[bf(c, "IDN")], drain=True)
    p.dma(lambda e: e.dma_start(out=sb["SC"][:, 0:128], in_=c.tri_dram.ap()),
          reads=[bf(c, "SC")], writes=[bf(c, "SC")], tag="c_tri")
    p.op("dve", lambda e: e.tensor_copy(sb["TRI"][:], sb["SC"][:, 0:128]),
         reads=[bf(c, "SC")], writes=[bf(c, "TRI")], drain=True)
    p.op("dve", lambda e: e.tensor_scalar_mul(sb["NEGI"][:], sb["IDN"][:], -30000.0),
         reads=[bf(c, "IDN")], writes=[bf(c, "NEGI")], drain=True)
    p.op("pool", lambda e: e.iota(sb["IOTA"][:], [[128, 16]], channel_multiplier=1,
                                  allow_small_or_imprecise_dtypes=True),
         writes=[bf(c, "IOTA")], drain=True)
    p.op("dve", lambda e: e.memset(sb["ONE1"][:], 1.0), writes=[bf(c, "ONE1")], drain=True)
    p.op("dve", lambda e: e.memset(sb["ONES2"][:], 0.0), writes=[bf(c, "ONES2")])
    p.op("dve", lambda e: e.memset(sb["ONES2"][0:64, 0:1], 1.0),
         writes=[bf(c, "ONES2")])
    p.op("dve", lambda e: e.memset(sb["ONES2"][64:128, 1:2], 1.0),
         writes=[bf(c, "ONES2")], drain=True)
    p.op("dve", lambda e: e.memset(sb["CN"][:, 0:1], EPS_LN), writes=[bf(c, "CN")])
    p.op("dve", lambda e: e.memset(sb["ONEC"][:], 1.0), writes=[bf(c, "ONEC")], drain=True)


def fp4_quant(c, src_name, src_ap, dst_ap, dst_buf, n, extra_reads=()):
    p, sb = c.p, c.sb
    nb = n // 16
    srcb = bf(c, src_name)
    r3 = lambda ap: ap.rearrange("p (a b) -> p a b", b=16)
    p.op("dve", lambda e: e.tensor_reduce(sb["SM"][:, 0:nb], r3(src_ap), AX.X,
                                          ALU.max, apply_absolute_value=True),
         reads=[srcb] + list(extra_reads), writes=[bf(c, "SM")], tag="fq_max")
    p.op("dve", lambda e: e.tensor_scalar(sb["SM2"][:, 0:nb], sb["SM"][:, 0:nb],
                                          1.0 / 6.0, 1e-12, ALU.mult, ALU.max),
         reads=[bf(c, "SM")], writes=[bf(c, "SM2")], tag="fq_s", drain=True)
    p.op("dve", lambda e: e.reciprocal(sb["SM"][:, 0:nb], sb["SM2"][:, 0:nb]),
         reads=[bf(c, "SM2")], writes=[bf(c, "SM")], tag="fq_inv")
    inv_bc = sb["SM"][:, 0:nb].rearrange("p a -> p a ()").broadcast_to([128, nb, 16])
    s_bc = sb["SM2"][:, 0:nb].rearrange("p a -> p a ()").broadcast_to([128, nb, 16])
    p.op("dve", lambda e: e.tensor_tensor(r3(sb["SC2"][:, 0:n]), r3(src_ap),
                                          inv_bc, ALU.mult),
         reads=[srcb, bf(c, "SM")] + list(extra_reads), writes=[bf(c, "SC2")],
         tag="fq_xn")
    p.op("dve", lambda e: e.tensor_single_scalar(sb["QI"][:, 0:n],
                                                 sb["SC2"][:, 0:n].bitcast(I32),
                                                 0x7FFFFFFF, ALU.bitwise_and),
         reads=[bf(c, "SC2")], writes=[bf(c, "QI")], tag="fq_i1")
    p.op("dve", lambda e: e.tensor_single_scalar(sb["QI"][:, 0:n], sb["QI"][:, 0:n],
                                                 0x00200000, ALU.add),
         reads=[bf(c, "QI")], writes=[bf(c, "QI")], tag="fq_i2")
    p.op("dve", lambda e: e.tensor_single_scalar(sb["QI2"][:, 0:n], sb["QI"][:, 0:n],
                                                 0xFFC00000 - (1 << 32),
                                                 ALU.bitwise_and),
         reads=[bf(c, "QI")], writes=[bf(c, "QI2")], tag="fq_i3")
    p.op("dve", lambda e: e.tensor_single_scalar(sb["QI2"][:, 0:n], sb["QI2"][:, 0:n],
                                                 0x3F800000, ALU.max),
         reads=[bf(c, "QI2")], writes=[bf(c, "QI2")], tag="fq_i4")
    p.op("act", lambda e: e.activation(sb["SC"][:, 0:n], sb["SC2"][:, 0:n], AF.Sign),
         reads=[bf(c, "SC2")], writes=[bf(c, "SC")], tag="fq_sgn")
    p.op("dve", lambda e: e.tensor_single_scalar(sb["SC2"][:, 0:n], sb["QI"][:, 0:n],
                                                 0x3F600000, ALU.is_gt),
         reads=[bf(c, "QI"), bf(c, "SC2")], writes=[bf(c, "SC2")], tag="fq_i75")
    p.op("dve", lambda e: e.tensor_single_scalar(sb["QI"][:, 0:n].bitcast(F32),
                                                 sb["QI"][:, 0:n], 0x3EA00000,
                                                 ALU.is_gt),
         reads=[bf(c, "QI")], writes=[bf(c, "QI")], tag="fq_i25")
    p.op("dve", lambda e: e.scalar_tensor_tensor(
        sb["QI2"][:, 0:n].bitcast(F32), sb["QI2"][:, 0:n].bitcast(F32), -0.5,
        sb["SC2"][:, 0:n], ALU.add, ALU.mult),
        reads=[bf(c, "QI2"), bf(c, "SC2")], writes=[bf(c, "QI2")], tag="fq_a")
    p.op("dve", lambda e: e.scalar_tensor_tensor(
        sb["SC2"][:, 0:n], sb["QI"][:, 0:n].bitcast(F32), 0.5,
        sb["QI2"][:, 0:n].bitcast(F32), ALU.mult, ALU.add),
        reads=[bf(c, "QI"), bf(c, "QI2"), bf(c, "SC2")], writes=[bf(c, "SC2")],
        tag="fq_b")
    p.op("pool", lambda e: e.tensor_tensor(r3(sb["SC"][:, 0:n]), r3(sb["SC"][:, 0:n]),
                                           s_bc, ALU.mult),
         reads=[bf(c, "SC"), bf(c, "SM2")], writes=[bf(c, "SC")], tag="fq_ss")
    p.op("dve", lambda e: e.tensor_tensor(dst_ap, sb["SC2"][:, 0:n],
                                          sb["SC"][:, 0:n], ALU.mult),
         reads=[bf(c, "SC2"), bf(c, "SC")], writes=[dst_buf], tag="fq_out")


def bcast_scalar(c, src_ap, dst_ap, dst_buf, src_buf):
    p, sb = c.p, c.sb
    p.op("dve", lambda e: e.tensor_copy(sb["BCAST"][0:1, 0:1], src_ap),
         reads=[src_buf], writes=[bf(c, "BCAST")], tag="bs_hi")
    p.op("dve", lambda e: e.tensor_copy(sb["ST"][0:1, 62:63], sb["BCAST"][0:1, 0:1]),
         reads=[bf(c, "BCAST")], writes=[bf(c, "ST")], tag="bs_hif")
    p.op("dve", lambda e: e.tensor_tensor(sb["ST"][0:1, 63:64], src_ap,
                                          sb["ST"][0:1, 62:63], ALU.subtract),
         reads=[src_buf, bf(c, "ST")], writes=[bf(c, "ST")], tag="bs_lo")
    p.op("dve", lambda e: e.tensor_copy(sb["BCAST"][0:1, 1:2], sb["ST"][0:1, 63:64]),
         reads=[bf(c, "ST")], writes=[bf(c, "BCAST")], tag="bs_lob", drain=True)
    p.op("pe", lambda e: e.matmul(c.ps["PO"][:, 1534:1536], sb["ONE1"][:],
                                  sb["BCAST"][0:1, 0:2], start=True, stop=True),
         reads=[bf(c, "ONE1"), bf(c, "BCAST")], writes=pob(c, 1534, 1536), tag="bs_mm")
    p.op("dve", lambda e: e.tensor_copy(sb["SM"][:, 44:46], c.ps["PO"][:, 1534:1536]),
         reads=pob(c, 1534, 1536), writes=[bf(c, "SM")], tag="bs_cp")
    p.op("dve", lambda e: e.tensor_tensor(dst_ap, sb["SM"][:, 44:45],
                                          sb["SM"][:, 45:46], ALU.add),
         reads=[bf(c, "SM")], writes=[dst_buf], tag="bs_add", drain=True)


def _transpose_1024(c, src_ap, src_buf, ncc):
    p = c.p
    for cc in range(ncc):
        p.op("pe", lambda e, cc=cc: e.transpose(
            c.ps["PX"][:, cc * 128:(cc + 1) * 128],
            src_ap[:, cc * 128:(cc + 1) * 128], c.sb["IDN"][:]),
            reads=[src_buf, bf(c, "IDN")], writes=[c.ps_buf["PX"]], tag="tr")


def emit_weight_prep(c):
    nc, p, sb = c.nc, c.p, c.sb
    phases = c.phases
    casts = []
    if 0 in phases:
        casts += [("L0_attn_q_w", "qb"), ("L0_attn_k_w", "kb"),
                  ("L0_attn_v_w", "vb"), ("L0_attn_o_w", "ob")]
    if 2 in phases:
        casts += [("L1_attn_Wq", "Wqb"), ("L1_attn_Wk", "Wkb"),
                  ("L1_attn_Wv", "Wvb"), ("L1_attn_Wo", "Wob")]
    pp = [0]
    for src, dst in casts:
        for t in range(NC):
            sl = slice(t * 128, (t + 1) * 128)
            wn = ("WB", "WB2")[pp[0] % 2]
            qn = ("QOUT", "QOUT2")[pp[0] % 2]
            pp[0] += 1
            p.dma(lambda e, src=src, sl=sl, wn=wn: e.dma_start(
                out=sb[wn][:], in_=c.w_in[src].ap()[sl, :]),
                reads=[bf(c, wn)], writes=[bf(c, wn)], tag="wp_ld")
            p.op("act", lambda e, wn=wn, qn=qn: e.activation(
                sb[qn][:], sb[wn][:], AF.Copy),
                reads=[bf(c, wn)], writes=[bf(c, qn)], tag="wp_cast")
            p.dma(lambda e, dst=dst, sl=sl, qn=qn: e.dma_start(
                out=c.ws[dst].ap()[sl, :], in_=sb[qn][:]),
                reads=[bf(c, qn)], writes=[c.hb[dst]], tag="wp_st")
    if 2 in phases:
        p.dma(lambda e: e.dma_start(out=sb["WB"][0:64, 0:M1],
                                    in_=c.w_in["L1_attn_random_matrix"].ap()),
              reads=[bf(c, "WB")], writes=[bf(c, "WB")])
        p.op("dve", lambda e: e.tensor_copy(sb["QOUT"][0:64, 0:M1],
                                            sb["WB"][0:64, 0:M1]),
             reads=[bf(c, "WB")], writes=[bf(c, "QOUT")])
        p.dma(lambda e: e.dma_start(out=c.ws["RMb"].ap(), in_=sb["QOUT"][0:64, 0:M1]),
              reads=[bf(c, "QOUT")], writes=[c.hb["RMb"]])
    if 1 in phases:
        for src, dst, rows, cols in [("L0_ffn_w_gate", "wgT0", I0, D),
                                     ("L0_ffn_w_up", "wuT0", I0, D),
                                     ("L0_ffn_w_down", "wdT0", D, I0)]:
            emit_fp4_weight_prep(c, src, dst, rows, cols)
    if 3 in phases:
        for src, dst, rows, cols in [("L1_ffn_w_gate", "wgT1", I1, D),
                                     ("L1_ffn_w_up", "wuT1", I1, D),
                                     ("L1_ffn_w_down", "wdT1", D, I1)]:
            emit_bitnet_weight_prep(c, src, dst, rows, cols)


def emit_fp4_weight_prep(c, src, dst, rows, cols):
    p, sb = c.p, c.sb
    ck = [0]
    for t in range(rows // 128):
        sl = slice(t * 128, (t + 1) * 128)
        for c0 in range(0, cols, 1024):
            cw = min(1024, cols - c0)
            wn = ("WB", "WB2")[ck[0] % 2]
            qn = ("QOUT", "QOUT2")[ck[0] % 2]
            hn = ("HQT", "HQT2")[ck[0] % 2]
            ck[0] += 1
            p.dma(lambda e, sl=sl, c0=c0, cw=cw, wn=wn: e.dma_start(
                out=sb[wn][:, 0:cw], in_=c.w_in[src].ap()[sl, c0:c0 + cw]),
                reads=[bf(c, wn)], writes=[bf(c, wn)], tag="f4w_ld")
            fp4_quant(c, wn, sb[wn][:, 0:cw], sb[qn][:, 0:cw],
                      bf(c, qn), cw)
            ncc = cw // 128
            _transpose_1024(c, sb[qn][:], bf(c, qn), ncc)
            p.op("act", lambda e, ncc=ncc, hn=hn: e.activation(
                sb[hn][:].rearrange("p a b -> p (a b)")[:, 0:ncc * 128],
                c.ps["PX"][:, 0:ncc * 128], AF.Copy),
                reads=[c.ps_buf["PX"]], writes=[bf(c, hn)], tag="f4w_ev")
            p.dma(lambda e, t=t, c0=c0, ncc=ncc, hn=hn: e.dma_start(
                out=c.ws[dst].ap().rearrange("(a q) r -> q a r", q=128)
                    [:, c0 // 128:c0 // 128 + ncc, t * 128:(t + 1) * 128],
                in_=sb[hn][:, 0:ncc, :]),
                reads=[bf(c, hn)], writes=[c.hb[dst]], tag="f4w_st")


def emit_bitnet_weight_prep(c, src, dst, rows, cols):
    p, sb = c.p, c.sb
    col = {"wgT1": 1, "wuT1": 2, "wdT1": 3}[dst]
    nrt = rows // 128
    p.op("dve", lambda e: e.memset(sb["ST"][:, 0:1], 0.0), writes=[bf(c, "ST")],
         tag="bnw_z")
    ck = [0]
    for t in range(nrt):
        sl = slice(t * 128, (t + 1) * 128)
        for c0 in range(0, cols, 1024):
            cw = min(1024, cols - c0)
            wn = ("WB", "WB2")[ck[0] % 2]
            ck[0] += 1
            p.dma(lambda e, sl=sl, c0=c0, cw=cw, wn=wn: e.dma_start(
                out=sb[wn][:, 0:cw], in_=c.w_in[src].ap()[sl, c0:c0 + cw]),
                reads=[bf(c, wn)], writes=[bf(c, wn)], tag="bnw_ld")
            p.op("dve", lambda e, cw=cw, wn=wn: e.tensor_reduce(
                sb["ST"][:, 1:2], sb[wn][:, 0:cw], AX.X, ALU.add,
                apply_absolute_value=True),
                reads=[bf(c, wn)], writes=[bf(c, "ST")], tag="bnw_red")
            p.op("dve", lambda e: e.tensor_tensor(sb["ST"][:, 0:1], sb["ST"][:, 0:1],
                                                  sb["ST"][:, 1:2], ALU.add),
                 reads=[bf(c, "ST")], writes=[bf(c, "ST")], tag="bnw_acc")
    p.op("dve", lambda e: e.tensor_scalar_mul(sb["SM"][:, 40:41], sb["ST"][:, 0:1],
                                              1.0 / (rows * cols)),
         reads=[bf(c, "ST")], writes=[bf(c, "SM")], tag="bnw_c")
    p.op("dve", lambda e: e.tensor_copy(sb["BCAST"][0:1, 2:3], sb["SM"][0:1, 40:41])
         if False else e.tensor_copy(sb["SM"][:, 41:42].bitcast(BF16)[:, 1:2],
                                     sb["SM"][:, 40:41]),
         reads=[bf(c, "SM")], writes=[bf(c, "SM")], tag="bnw_cb16", drain=True)
    p.op("pe", lambda e: e.matmul(c.ps["PO"][0:1, 256:257],
                                  sb["SM"][:, 41:42].bitcast(BF16)[:, 1:2],
                                  sb["ONEC"][:], start=True, stop=True),
         reads=[bf(c, "SM"), bf(c, "ONEC")], writes=pob(c, 256, 257), tag="bnw_mm")
    p.op("dve", lambda e: e.tensor_scalar_max(sb["ST"][0:1, 4:5],
                                              c.ps["PO"][0:1, 256:257], 1e-5),
         reads=pob(c, 256, 257), writes=[bf(c, "ST")], tag="bnw_mean")
    p.op("dve", lambda e: e.reciprocal(sb["ST"][0:1, 5:6], sb["ST"][0:1, 4:5]),
         reads=[bf(c, "ST")], writes=[bf(c, "ST")], tag="bnw_rcp")
    bcast_scalar(c, sb["ST"][0:1, 5:6], sb["SM"][:, 8:9], bf(c, "SM"), bf(c, "ST"))
    bcast_scalar(c, sb["ST"][0:1, 4:5], sb["CN"][:, col:col + 1], bf(c, "CN"),
                 bf(c, "ST"))
    ck = [0]
    for t in range(nrt):
        sl = slice(t * 128, (t + 1) * 128)
        for c0 in range(0, cols, 1024):
            cw = min(1024, cols - c0)
            wn = ("WB", "WB2")[ck[0] % 2]
            qn = ("QOUT", "QOUT2")[ck[0] % 2]
            hn = ("HQT", "HQT2")[ck[0] % 2]
            ck[0] += 1
            p.dma(lambda e, sl=sl, c0=c0, cw=cw, wn=wn: e.dma_start(
                out=sb[wn][:, 0:cw], in_=c.w_in[src].ap()[sl, c0:c0 + cw]),
                reads=[bf(c, wn)], writes=[bf(c, wn)], tag="bnw_ld2")
            p.op("dve", lambda e, cw=cw, wn=wn: e.tensor_single_scalar(
                sb["SC2"][:, 0:cw], sb[wn][:, 0:cw], sb["SM"][:, 8:9], ALU.mult),
                reads=[bf(c, wn), bf(c, "SM")], writes=[bf(c, "SC2")],
                tag="bnw_m0")
            p.op("dve", lambda e, cw=cw: e.tensor_scalar(
                sb["SC2"][:, 0:cw], sb["SC2"][:, 0:cw],
                -1.0, 1.0, ALU.max, ALU.min),
                reads=[bf(c, "SC2")], writes=[bf(c, "SC2")], tag="bnw_clf")
            p.op("dve", lambda e, cw=cw: e.tensor_scalar_add(
                sb["SC2"][:, 0:cw], sb["SC2"][:, 0:cw], MAGIC),
                reads=[bf(c, "SC2")], writes=[bf(c, "SC2")], tag="bnw_m")
            p.op("dve", lambda e, cw=cw: e.tensor_single_scalar(
                sb["QI"][:, 0:cw], sb["SC2"][:, 0:cw].bitcast(I32),
                0, ALU.bitwise_or),
                reads=[bf(c, "SC2")], writes=[bf(c, "QI")], tag="bnw_cl")
            p.op("dve", lambda e, cw=cw, qn=qn: e.tensor_scalar_add(
                sb[qn][:, 0:cw], sb["QI"][:, 0:cw].bitcast(F32), -MAGIC),
                reads=[bf(c, "QI")], writes=[bf(c, qn)], tag="bnw_b")
            ncc = cw // 128
            _transpose_1024(c, sb[qn][:], bf(c, qn), ncc)
            p.op("act", lambda e, ncc=ncc, hn=hn: e.activation(
                sb[hn][:].rearrange("p a b -> p (a b)")[:, 0:ncc * 128],
                c.ps["PX"][:, 0:ncc * 128], AF.Copy),
                reads=[c.ps_buf["PX"]], writes=[bf(c, hn)], tag="bnw_ev")
            p.dma(lambda e, t=t, c0=c0, ncc=ncc, hn=hn: e.dma_start(
                out=c.ws[dst].ap().rearrange("(a q) r -> q a r", q=128)
                    [:, c0 // 128:c0 // 128 + ncc, t * 128:(t + 1) * 128],
                in_=sb[hn][:, 0:ncc, :]),
                reads=[bf(c, hn)], writes=[c.hb[dst]], tag="bnw_st")


def emit_fast_rsqrt(c, dst_ap, src_ap, n, scratch_off=48):
    p, sb = c.p, c.sb
    so = scratch_off
    p.op("dve", lambda e: e.tensor_single_scalar(sb["QI"][:, 0:n],
                                                 src_ap.bitcast(I32), 1,
                                                 ALU.arith_shift_right),
         reads=[], writes=[bf(c, "QI")], tag="rs_sh0")
    p.op("dve", lambda e: e.tensor_single_scalar(sb["QI"][:, 0:n], sb["QI"][:, 0:n],
                                                 0x7FFFFFFF, ALU.bitwise_xor),
         reads=[bf(c, "QI")], writes=[bf(c, "QI")], tag="rs_x")
    p.op("dve", lambda e: e.tensor_single_scalar(sb["QI"][:, 0:n], sb["QI"][:, 0:n],
                                                 0x5F3759DF - 0x7FFFFFFF, ALU.add),
         reads=[bf(c, "QI")], writes=[bf(c, "QI")], tag="rs_a")
    y = sb["QI"][:, 0:n].bitcast(F32)
    for it in range(2):
        p.op("dve", lambda e: e.tensor_tensor(sb["ST2"][:, so:so + n], y, y,
                                              ALU.mult),
             reads=[bf(c, "QI")], writes=[bf(c, "ST2")], tag="rs_y2")
        p.op("dve", lambda e: e.tensor_tensor(sb["ST2"][:, so:so + n],
                                              sb["ST2"][:, so:so + n], src_ap,
                                              ALU.mult),
             reads=[bf(c, "ST2")], writes=[bf(c, "ST2")], tag="rs_vy2")
        p.op("dve", lambda e: e.tensor_scalar(sb["ST2"][:, so:so + n],
                                              sb["ST2"][:, so:so + n], -0.5, 1.5,
                                              ALU.mult, ALU.add),
             reads=[bf(c, "ST2")], writes=[bf(c, "ST2")], tag="rs_c")
        p.op("dve", lambda e: e.tensor_tensor(y, y, sb["ST2"][:, so:so + n],
                                              ALU.mult),
             reads=[bf(c, "ST2"), bf(c, "QI")], writes=[bf(c, "QI")], tag="rs_y")
    p.op("dve", lambda e: e.tensor_copy(dst_ap, y),
         reads=[bf(c, "QI")], writes=[bf(c, "ST")], tag="rs_out")


def emit_ln_stats(c, src_tile_fn, reads_fn):
    p, sb = c.p, c.sb
    for t in range(NT):
        p.op("dve", lambda e, t=t: e.reduce_sum(sb["ST"][:, t:t + 1],
                                                src_tile_fn(t), AX.X),
             reads=reads_fn(t), writes=[bf(c, "ST")], tag="ln_sum")
        p.op("dve", lambda e, t=t: e.scalar_tensor_tensor(
            sb["SC2"][:], src_tile_fn(t), 0.0, src_tile_fn(t), ALU.bypass,
            ALU.mult),
            reads=reads_fn(t), writes=[bf(c, "SC2")], tag="ln_sq")
        p.op("dve", lambda e, t=t: e.reduce_sum(sb["ST"][:, 32 + t:33 + t],
                                                sb["SC2"][:], AX.X),
             reads=[bf(c, "SC2")], writes=[bf(c, "ST")], tag="ln_sqr")
    p.op("dve", lambda e: e.tensor_scalar_mul(sb["ST"][:, 0:16], sb["ST"][:, 0:16],
                                              1.0 / D),
         reads=[bf(c, "ST")], writes=[bf(c, "ST")], tag="ln_mean")
    p.op("dve", lambda e: e.tensor_scalar_mul(sb["ST2"][:, 16:32], sb["ST"][:, 32:48],
                                              1.0 / D),
         reads=[bf(c, "ST")], writes=[bf(c, "ST2")], tag="ln_e2")
    p.op("dve", lambda e: e.tensor_tensor(sb["ST2"][:, 32:48], sb["ST"][:, 0:16],
                                          sb["ST"][:, 0:16], ALU.mult),
         reads=[bf(c, "ST")], writes=[bf(c, "ST2")], tag="ln_m2")
    p.op("dve", lambda e: e.tensor_tensor(sb["ST2"][:, 16:32], sb["ST2"][:, 16:32],
                                          sb["ST2"][:, 32:48], ALU.subtract),
         reads=[bf(c, "ST2")], writes=[bf(c, "ST2")], tag="ln_var")
    p.op("dve", lambda e: e.tensor_scalar_add(sb["ST2"][:, 16:32],
                                              sb["ST2"][:, 16:32], EPS_LN),
         reads=[bf(c, "ST2")], writes=[bf(c, "ST2")], tag="ln_vep")
    emit_fast_rsqrt(c, sb["ST"][:, 16:32], sb["ST2"][:, 16:32], 16)
    p.op("dve", lambda e: e.scalar_tensor_tensor(sb["ST2"][:, 0:16], sb["ST"][:, 0:16],
                                                 -1.0, sb["ST"][:, 16:32],
                                                 ALU.mult, ALU.mult),
         reads=[bf(c, "ST")], writes=[bf(c, "ST2")], tag="ln_nmr")


def emit_ln_apply(c, t, src_ap, dst_ap, reads, writes):
    p, sb = c.p, c.sb
    p.op("dve", lambda e: e.tensor_scalar(dst_ap, src_ap, sb["ST"][:, 16 + t:17 + t],
                                          sb["ST2"][:, t:t + 1], ALU.mult, ALU.add),
         reads=list(reads) + [bf(c, "ST"), bf(c, "ST2")], writes=list(writes),
         tag="ln_app")


def emit_xt_from_bf16_tiles(c, t, src_ap, src_buf):
    p, sb = c.p, c.sb
    _transpose_1024(c, src_ap, src_buf, 8)
    p.op("dve", lambda e, t=t: e.tensor_copy(
        sb["XT"][:, :, t * 128:(t + 1) * 128],
        c.ps["PX"][:].rearrange("p (a b) -> p a b", b=128)),
        reads=[c.ps_buf["PX"]], writes=[bf(c, "XT")], tag="xt_ev")


def emit_ln_to_xt(c, reads_extra=()):
    sb = c.sb
    emit_ln_stats(c, lambda t: sb["X"][:, t, :], lambda t: [bf(c, "X")])
    dbg_snap(c, "ln_st", "ST")
    dbg_snap(c, "ln_st2", "ST2")
    for t in range(NT):
        emit_ln_apply(c, t, sb["X"][:, t, :], sb["QOUT"][:],
                      reads=[bf(c, "X")], writes=[bf(c, "QOUT")])
        if t == 8:
            dbg_snap(c, "qout_t8", "QOUT")
        emit_xt_from_bf16_tiles(c, t, sb["QOUT"][:], bf(c, "QOUT"))


def emit_load_x(c, seq, x_loc):
    p, sb = c.p, c.sb
    if x_loc == "hbm_in":
        src = c.x_in.ap()[seq].rearrange("(t p) d -> p t d", p=128)
        hbuf = c.hb["x"]
    else:
        src = c.xs.ap().rearrange("(t p) d -> p t d", p=128)
        hbuf = c.hb["xs"]
    p.dma(lambda e: e.dma_start(out=sb["X"][:], in_=src),
          reads=[hbuf], writes=[bf(c, "X")], tag="x_ld")


def _load_wslice(c, ws_name, pair, dst_name):
    p, sb = c.p, c.sb
    src = c.ws[ws_name].ap().rearrange("(a q) r -> q a r", q=128)[
        :, :, pair * 128:(pair + 1) * 128]
    p.dma(lambda e: e.dma_start(out=sb[dst_name][:], in_=src),
          reads=[c.hb[ws_name], bf(c, dst_name)], writes=[bf(c, dst_name)],
          tag="wsl")


def _proj_fm(c, wbuf_name, dst_name, dst_buf, evac_eng, scale=1.0, bias_col=None,
             extra_evac_reads=()):
    p, sb = c.p, c.sb
    for qc in range(4):
        quarter = qc % 2
        qsl = slice(quarter * 512, (quarter + 1) * 512)
        for kc in range(NC):
            p.op("pe", lambda e, qc=qc, kc=kc, qsl=qsl: e.matmul(
                c.ps["PSS"][:, qsl], sb[wbuf_name][:, kc, :],
                sb["XT"][:, kc, qc * 512:(qc + 1) * 512],
                start=(kc == 0), stop=(kc == NC - 1)),
                reads=[bf(c, wbuf_name), bf(c, "XT")],
                writes=[c.ps_buf["PSS%d" % quarter]], tag="prj_mm")
        if bias_col is None:
            em = lambda e, qc=qc, qsl=qsl: e.activation(
                sb[dst_name][:, qc * 512:(qc + 1) * 512], c.ps["PSS"][:, qsl],
                AF.Copy, scale=scale)
            eng = evac_eng
        else:
            em = lambda e, qc=qc, qsl=qsl: e.tensor_scalar_add(
                sb[dst_name][:, qc * 512:(qc + 1) * 512], c.ps["PSS"][:, qsl],
                bias_col)
            eng = "dve"
        p.op(eng, em,
             reads=[c.ps_buf["PSS%d" % (qc % 2)]] + list(extra_evac_reads),
             writes=[dst_buf], tag="prj_ev")


def _proj_v_tokmajor(c, wbuf_name, pair, bias_col=None):
    p, sb = c.p, c.sb
    _proj_fm(c, wbuf_name, "PT", bf(c, "PT"), "act", bias_col=bias_col,
             extra_evac_reads=[bf(c, "BCOL")] if bias_col is not None else ())
    if P2_TRUNC == 15:
        return
    for g in range(2):
        for j in range(8):
            t = g * 8 + j
            p.op("pe", lambda e, t=t, j=j: e.transpose(
                c.ps["PX"][:, j * 128:(j + 1) * 128],
                sb["PT"][:, t * 128:(t + 1) * 128], sb["IDN"][:]),
                reads=[bf(c, "PT"), bf(c, "IDN")], writes=[c.ps_buf["PX"]],
                tag="v_tr")
        p.op("dve", lambda e, g=g: e.tensor_copy(
            sb["VP"][:, :, g * 8:(g + 1) * 8, 0:HD],
            c.ps["PX"][:].rearrange("p (t h d) -> p h t d", t=8, h=2),
            ),
            reads=[c.ps_buf["PX"]], writes=[bf(c, "VP")], tag="v_ev")
    p.op("dve", lambda e: e.memset(sb["VP"][:, :, :, HD:HD + 1], 1.0),
         writes=[bf(c, "VP")], tag="v_one", drain=True)


def emit_p0(c, seq, x_loc, is_last):
    p, sb = c.p, c.sb
    claim(c, "X", "OW", "QP", "KP", "VP", "WS1", "WS2", "PT", "NIO", "QH", "KH",
          "ROWSC")
    p.op("pool", lambda e: e.iota(sb["NIO"][64:65, :], [[-1, S]],
                                  channel_multiplier=0,
                                  allow_small_or_imprecise_dtypes=True),
         reads=[bf(c, "NIO")], writes=[bf(c, "NIO")], tag="c_nio", drain=True)
    emit_load_x(c, seq, x_loc)
    emit_ln_to_xt(c)
    dbg_snap(c, "xt_ln", "XT")
    p.dma(lambda e: e.dma_start(
        out=sb["OW"][:], in_=c.ws["ob"].ap().rearrange("(a q) r -> q a r", q=128)),
        reads=[c.hb["ob"], bf(c, "OW")], writes=[bf(c, "OW")], tag="ow_ld")
    for pair in range(NPAIR):
        _load_wslice(c, "qb", pair, "WS1")
        if pair == 0:
            dbg_snap(c, "ws1_p0", "WS1")
        _proj_fm(c, "WS1", "QP", bf(c, "QP"), "act", scale=0.125)
        if pair == 0:
            dbg_snap(c, "qp_p0", "QP")
        _load_wslice(c, "kb", pair, "WS2")
        _proj_fm(c, "WS2", "KP", bf(c, "KP"), "act")
        _load_wslice(c, "vb", pair, "WS1")
        _proj_v_tokmajor(c, "WS1", pair)
        emit_pair_norm_bound(c)
        for half in (1, 0):
            emit_head_attention(c, seq, pair, half)
    emit_out_proj(c, target="X", bias_row=None)
    return "X"


def emit_pair_norm_bound(c):
    p, sb = c.p, c.sb
    for j, buf in enumerate(("QP", "KP")):
        p.op("act", lambda e, buf=buf: e.activation(sb["PT"][:], sb[buf][:],
                                                    AF.Square),
             reads=[bf(c, buf)], writes=[bf(c, "PT")], tag="nb_sq")
        for ch in range(4):
            p.op("pe", lambda e, ch=ch: e.matmul(
                c.ps["PSS"][0:2, ch * 512:(ch + 1) * 512],
                sb["ONES2"][:], sb["PT"][:, ch * 512:(ch + 1) * 512],
                start=True, stop=True),
                reads=[bf(c, "ONES2"), bf(c, "PT")],
                writes=[c.ps_buf["PSS%d" % ch]], tag="nb_mm")
        p.op("dve", lambda e, j=j: e.tensor_reduce(
            sb["NRM"][0:2, j:j + 1], c.ps["PSS"][0:2, :], AX.X, ALU.max),
            reads=[c.ps_buf["PSS%d" % q] for q in range(4)],
            writes=[bf(c, "NRM")], tag="nb_red")
    p.op("dve", lambda e: e.tensor_tensor(sb["NRM"][0:2, 2:3], sb["NRM"][0:2, 0:1],
                                          sb["NRM"][0:2, 1:2], ALU.mult),
         reads=[bf(c, "NRM")], writes=[bf(c, "NRM")], tag="nb_p")
    p.op("dve", lambda e: e.tensor_copy(sb["MCOL"][0:2, 0:1], sb["NRM"][0:2, 2:3]),
         reads=[bf(c, "NRM")], writes=[bf(c, "MCOL")], tag="nb_cb", drain=True)
    p.op("pe", lambda e: e.transpose(c.ps["PX"][0:1, 0:2], sb["MCOL"][0:2, 0:1],
                                     sb["IDN"][0:2, 0:2]),
         reads=[bf(c, "MCOL"), bf(c, "IDN")], writes=[c.ps_buf["PX"]], tag="nb_tr")
    p.op("dve", lambda e: e.tensor_reduce(sb["NRM"][0:1, 4:5], c.ps["PX"][0:1, 0:2],
                                          AX.X, ALU.max),
         reads=[c.ps_buf["PX"]], writes=[bf(c, "NRM")], tag="nb_mx")
    p.op("act", lambda e: e.activation(sb["NRM"][0:1, 5:6], sb["NRM"][0:1, 4:5],
                                       AF.Sqrt),
         reads=[bf(c, "NRM")], writes=[bf(c, "NRM")], tag="nb_sqrt")
    p.op("dve", lambda e: e.tensor_scalar(sb["NRM"][0:1, 6:7], sb["NRM"][0:1, 5:6],
                                          1.0, 8.0, ALU.mult, ALU.add),
         reads=[bf(c, "NRM")], writes=[bf(c, "NRM")], tag="nb_m")
    p.op("dve", lambda e: e.tensor_copy(sb["MCOL"][0:1, 1:2], sb["NRM"][0:1, 6:7]),
         reads=[bf(c, "NRM")], writes=[bf(c, "MCOL")], tag="nb_cb2", drain=True)
    p.op("pe", lambda e: e.matmul(c.ps["PO"][:, 256:257], sb["ONE1"][:],
                                  sb["MCOL"][0:1, 1:2], start=True, stop=True),
         reads=[bf(c, "MCOL"), bf(c, "ONE1")], writes=pob(c, 256, 257), tag="nb_bc")
    p.op("dve", lambda e: e.tensor_scalar_mul(sb["MB"][:], c.ps["PO"][:, 256:257],
                                              -1.0),
         reads=pob(c, 256, 257), writes=[bf(c, "MB")], tag="nb_neg")


def emit_head_attention(c, seq, pair, half):
    p, sb = c.p, c.sb
    h = 2 * pair + half
    slope = SLOPES[h]
    import numpy as _np
    import ml_dtypes as _md
    s_hi = float(_np.float32(_md.bfloat16(slope)))
    s_lo = float(_np.float32(_md.bfloat16(slope - s_hi)))
    if half == 1:
        for dst, srcname in (("QH", "QP"), ("KH", "KP")):
            p.dma(lambda e, dst=dst, srcname=srcname: e.dma_start(
                out=sb[dst][0:64, :], in_=sb[srcname][64:128, :]),
                reads=[bf(c, srcname), bf(c, dst)], writes=[bf(c, dst)],
                tag="ah_cp")
        qhat_n, khat_n = "QH", "KH"
    else:
        qhat_n, khat_n = "QP", "KP"
    qhat, khat = sb[qhat_n], sb[khat_n]
    qb_, kb_ = bf(c, qhat_n), bf(c, khat_n)
    p.op("dve", lambda e: e.tensor_copy(qhat[64:65, :], sb["NIO"][64:65, :]),
         reads=[bf(c, "NIO"), qb_], writes=[qb_], tag="ah_q64")
    p.op("dve", lambda e: e.memset(khat[64:65, :], slope),
         reads=[kb_], writes=[kb_], tag="ah_k64", drain=True)
    p.op("pool", lambda e: e.iota(
        sb["ROWSC"][0:1, :].rearrange("o (a b) -> o a b", b=128),
        [[128, 16], [0, 128]], channel_multiplier=0,
        allow_small_or_imprecise_dtypes=True),
        reads=[bf(c, "ROWSC")], writes=[bf(c, "ROWSC")], tag="ah_mk")
    p.op("dve", lambda e: e.tensor_scalar_add(sb["ROWSC"][0:1, :],
                                              sb["ROWSC"][0:1, :], 64.0),
         reads=[bf(c, "ROWSC")], writes=[bf(c, "ROWSC")], tag="ah_mk2",
         drain=True)
    for rr in (65, 66):
        p.dma(lambda e, rr=rr: e.dma_start(out=khat[rr:rr + 1, :],
                                           in_=sb["ROWSC"][0:1, :]),
              reads=[bf(c, "ROWSC"), kb_], writes=[kb_], tag="ah_krow")
    for rr, val in ((65, s_hi), (66, s_lo)):
        p.op("dve", lambda e, val=val: e.memset(sb["ROWSC"][0:1, :], val),
             reads=[bf(c, "ROWSC")], writes=[bf(c, "ROWSC")], tag="ah_sv",
             drain=True)
        p.dma(lambda e, rr=rr: e.dma_start(out=qhat[rr:rr + 1, :],
                                           in_=sb["ROWSC"][0:1, :]),
              reads=[bf(c, "ROWSC"), qb_], writes=[qb_], tag="ah_qrow")
    p.op("dve", lambda e: e.tensor_scalar(sb["BIAS"][:, 0:1], sb["IOTA"][:, 0:1],
                                          slope, -64.0 * slope, ALU.mult,
                                          ALU.add),
         reads=[bf(c, "IOTA")], writes=[bf(c, "BIAS")], tag="at_bias0")
    p.op("dve", lambda e: e.tensor_tensor(sb["BIAS"][:, 0:1], sb["BIAS"][:, 0:1],
                                          sb["MB"][:], ALU.add),
         reads=[bf(c, "BIAS"), bf(c, "MB")], writes=[bf(c, "BIAS")],
         tag="at_bias", drain=True)
    def ptn_of(it):
        return "PT" if it % 2 == 0 else "PT2"

    def emit_pv(jt):
        ptn = ptn_of(jt)
        for kb in range(jt + 1):
            p.op("pe", lambda e, kb=kb, jt=jt, half=half, ptn=ptn: e.matmul(
                c.ps["PO"][:, po_off(jt):po_off(jt) + 65],
                sb[ptn][:, kb * 128:(kb + 1) * 128],
                sb["VP"][:, half, kb, :],
                start=(kb == 0), stop=(kb == jt)),
                reads=[bf(c, ptn), bf(c, "VP")],
                writes=pob(c, po_off(jt), po_off(jt) + 65), tag="at_pv")

    for it in range(NT):
        ptn = ptn_of(it)
        for kb in range(it + 1):
            wbuf = [c.ps_buf["PSS%d" % (kb // 4)]]
            p.op("pe", lambda e, kb=kb, it=it: e.matmul(
                c.ps["PSS"][:, kb * 128:(kb + 1) * 128],
                khat[0:67, kb * 128:(kb + 1) * 128],
                qhat[0:67, it * 128:(it + 1) * 128],
                start=True, stop=(kb != it)),
                reads=[kb_, qb_], writes=wbuf, tag="at_qk")
        p.op("pe", lambda e, it=it: e.matmul(
            c.ps["PSS"][:, it * 128:(it + 1) * 128], sb["TRI"][:], sb["NEGI"][:],
            start=False, stop=True),
            reads=[bf(c, "TRI"), bf(c, "NEGI")],
            writes=[c.ps_buf["PSS%d" % (it // 4)]], tag="at_tri")
        for b in range(it // 8 + 1):
            c0 = b * 1024
            c1 = min((it + 1) * 128, c0 + 1024)
            rb = [c.ps_buf["PSS%d" % q]
                  for q in range(c0 // 512, (c1 + 511) // 512)]
            p.op("act", lambda e, c0=c0, c1=c1, ptn=ptn: e.activation(
                sb[ptn][:, c0:c1], c.ps["PSS"][:, c0:c1],
                AF.Exp, bias=sb["BIAS"][:, 0:1]),
                reads=rb + [bf(c, "BIAS")],
                writes=[bf(c, ptn)], tag="at_exp")
        if it > 0:
            emit_pv(it - 1)
        if it - 1 == 6:
            emit_attn_norm_evac(c, h, groups=[0])
        if it - 1 == 13:
            emit_attn_norm_evac(c, h, groups=[1])
    emit_pv(NT - 1)
    emit_attn_norm_evac(c, h, groups=[2])


def emit_attn_norm_evac(c, h, clamp=None, groups=None):
    p, sb = c.p, c.sb
    allg = [(0, 7), (7, 14), (14, 16)]
    sel = allg if groups is None else [allg[g] for g in groups]
    for (i0, i1) in sel:
        n = i1 - i0
        sums = c.ps["PO"][:, po_off(i0):po_off(i0) + n * 65]\
            .rearrange("p (t u) -> p t u", u=65)[:, :, 64:65]\
            .rearrange("p t o -> p (t o)")
        if clamp is None:
            p.op("dve", lambda e, i0=i0, n=n, sums=sums: e.reciprocal(
                sb["ST2"][:, 48 + i0:48 + i0 + n], sums),
                reads=pob(c, po_off(i0), po_off(i0) + n * 65),
                writes=[bf(c, "ST2")], tag="at_rcp")
        else:
            p.op("dve", lambda e, i0=i0, n=n, sums=sums: e.tensor_scalar_max(
                sb["ST2"][:, 48 + i0:48 + i0 + n], sums, clamp),
                reads=pob(c, po_off(i0), po_off(i0) + n * 65),
                writes=[bf(c, "ST2")], tag="at_cl")
            p.op("dve", lambda e, i0=i0, n=n: e.reciprocal(
                sb["ST2"][:, 48 + i0:48 + i0 + n],
                sb["ST2"][:, 48 + i0:48 + i0 + n]),
                reads=[bf(c, "ST2")], writes=[bf(c, "ST2")], tag="at_rcp")
        p.op("dve", lambda e, i0=i0, n=n, h=h: e.tensor_tensor(
            sb["BB"][:, i0:i0 + n, h * 64:(h + 1) * 64],
            c.ps["PO"][:, po_off(i0):po_off(i0) + n * 65]
            .rearrange("p (t u) -> p t u", u=65)[:, :, 0:64],
            sb["ST2"][:, 48 + i0:48 + i0 + n].rearrange("p t -> p t ()")
            .broadcast_to([128, n, 64]), ALU.mult),
            reads=pob(c, po_off(i0), po_off(i0) + n * 65) + [bf(c, "ST2")],
            writes=[bf(c, "BB")], tag="at_nrm", drain=True)


def emit_out_proj(c, target, bias_row):
    p, sb = c.p, c.sb
    for t in range(NT):
        emit_xt_from_bf16_tiles(c, t, sb["BB"][:, t, :], bf(c, "BB"))
    for t in range(NT):
        for nh in range(2):
            qsl = slice(nh * 512, nh * 512 + 512)
            for kc in range(NC):
                p.op("pe", lambda e, t=t, kc=kc, nh=nh, qsl=qsl: e.matmul(
                    c.ps["PSS"][:, qsl], sb["XT"][:, kc, t * 128:(t + 1) * 128],
                    sb["OW"][:, kc, nh * 512:(nh + 1) * 512],
                    start=(kc == 0), stop=(kc == NC - 1 and bias_row is None)),
                    reads=[bf(c, "XT"), bf(c, "OW")],
                    writes=[c.ps_buf["PSS%d" % nh]], tag="op_mm")
            if bias_row is not None:
                p.op("pe", lambda e, nh=nh, qsl=qsl: e.matmul(
                    c.ps["PSS"][:, qsl], sb["ONE1"][:],
                    sb["ROWB"][0:1, nh * 512:(nh + 1) * 512],
                    start=False, stop=True),
                    reads=[bf(c, "ONE1"), bf(c, "ROWB")],
                    writes=[c.ps_buf["PSS%d" % nh]], tag="op_bias")
        p.op("dve", lambda e, t=t: e.tensor_tensor(
            sb["X"][:, t, :], c.ps["PSS"][:, 0:1024], sb["X"][:, t, :], ALU.add),
            reads=[c.ps_buf["PSS0"], c.ps_buf["PSS1"], bf(c, "X")],
            writes=[bf(c, "X")], tag="op_res")


def emit_p1(c, seq, x_loc, is_last):
    p, sb = c.p, c.sb
    claim(c, "X", "PT")
    if x_loc != "X":
        emit_load_x(c, seq, x_loc)
    emit_ln_stats(c, lambda t: sb["X"][:, t, :], lambda t: [bf(c, "X")])
    for t in range(NT):
        emit_ln_apply(c, t, sb["X"][:, t, :], sb["SC"][:],
                      reads=[bf(c, "X")], writes=[bf(c, "SC")])
        fp4_quant(c, "SC", sb["SC"][:], sb["QOUT"][:], bf(c, "QOUT"), 1024)
        if DEBUG_SNAP and t == 0 and seq == 0:
            dbg_snap(c, "xq_t0", "QOUT")
        emit_xt_from_bf16_tiles(c, t, sb["QOUT"][:], bf(c, "QOUT"))
    p.dma(lambda e: e.dma_start(
        out=c.xs.ap().rearrange("(t p) d -> p t d", p=128), in_=sb["X"][:]),
        reads=[bf(c, "X")], writes=[c.hb["xs"]], tag="x_park")
    claim(c, "WG0", "WDb0", "WU0", "WDa0")
    p.dma(lambda e: e.dma_start(
        out=sb["WG0"][:], in_=c.ws["wgT0"].ap().rearrange("(a q) r -> q a r", q=128)),
        reads=[c.hb["wgT0"]], writes=[bf(c, "WG0")], tag="wg0_ld")
    p.dma(lambda e: e.dma_start(
        out=sb["WU0"][:], in_=c.ws["wuT0"].ap().rearrange("(a q) r -> q a r", q=128)),
        reads=[c.hb["wuT0"]], writes=[bf(c, "WU0")], tag="wu0_ld")
    p.dma(lambda e: e.dma_start(
        out=sb["WDa0"][:],
        in_=c.ws["wdT0"].ap().rearrange("(a q) r -> q a r", q=128)[:, 0:16, :]),
        reads=[c.hb["wdT0"]], writes=[bf(c, "WDa0")], tag="wda_ld")
    p.dma(lambda e: e.dma_start(
        out=sb["WDb0"][:],
        in_=c.ws["wdT0"].ap().rearrange("(a q) r -> q a r", q=128)[:, 16:22, :]),
        reads=[c.hb["wdT0"]], writes=[bf(c, "WDb0")], tag="wdb_ld")
    ccs = [(0, 1024), (1024, 1024), (2048, 768)]
    for t in range(NT):
        for ci, (c0, cw) in enumerate(ccs):
            for h0 in range(0, cw, 512):
                qsl = slice(h0, min(h0 + 512, cw))
                half = h0 // 512
                for kc in range(NC):
                    p.op("pe", lambda e, t=t, kc=kc, c0=c0, qsl=qsl: e.matmul(
                        c.ps["PSS"][:, qsl], sb["XT"][:, kc, t * 128:(t + 1) * 128],
                        sb["WG0"][:, kc, c0 + qsl.start:c0 + qsl.stop],
                        start=(kc == 0), stop=(kc == NC - 1)),
                        reads=[bf(c, "XT"), bf(c, "WG0")],
                        writes=[c.ps_buf["PSS%d" % half]], tag="g_mm")
                for kc in range(NC):
                    p.op("pe", lambda e, t=t, kc=kc, c0=c0, qsl=qsl: e.matmul(
                        c.ps["PO"][:, qsl], sb["XT"][:, kc, t * 128:(t + 1) * 128],
                        sb["WU0"][:, kc, c0 + qsl.start:c0 + qsl.stop],
                        start=(kc == 0), stop=(kc == NC - 1)),
                        reads=[bf(c, "XT"), bf(c, "WU0")],
                        writes=pob(c, qsl.start, qsl.stop), tag="u_mm")
            pss_r = [c.ps_buf["PSS%d" % q] for q in range((cw + 511) // 512)]
            p.op("act", lambda e, cw=cw: e.activation(sb["SC"][:, 0:cw],
                                                      c.ps["PSS"][:, 0:cw], AF.Silu),
                 reads=pss_r, writes=[bf(c, "SC")], tag="g_silu")
            p.op("dve", lambda e, cw=cw: e.tensor_tensor(
                sb["WB"][:, 0:cw], c.ps["PO"][:, 0:cw], sb["SC"][:, 0:cw], ALU.mult),
                reads=pob(c, 0, cw) + [bf(c, "SC")], writes=[bf(c, "WB")],
                tag="h_mul")
            if DEBUG_SNAP and t == 0 and ci == 0 and seq == 0:
                dbg_snap(c, "h_t0", "WB")
            fp4_quant(c, "WB", sb["WB"][:, 0:cw], sb["QOUT"][:, 0:cw],
                      bf(c, "QOUT"), cw)
            if DEBUG_SNAP and t == 0 and ci == 0 and seq == 0:
                dbg_snap(c, "hq_t0", "QOUT")
            _transpose_1024(c, sb["QOUT"][:], bf(c, "QOUT"), cw // 128)
            p.op("dve", lambda e, cw=cw: e.tensor_copy(
                sb["HQT"][:].rearrange("p a b -> p (a b)")[:, 0:cw],
                c.ps["PX"][:, 0:cw]),
                reads=[c.ps_buf["PX"]], writes=[bf(c, "HQT")], tag="hq_ev", drain=True)
            for j in range(cw // 128):
                kcg = c0 // 128 + j
                wt = "WDa0" if kcg < 16 else "WDb0"
                wj = kcg if kcg < 16 else kcg - 16
                for nh in range(2):
                    p.op("pe", lambda e, j=j, wt=wt, wj=wj, nh=nh, ci=ci, t=t: e.matmul(
                        c.ps["PSS"][:, 1024 + nh * 512:1024 + nh * 512 + 512],
                        sb["HQT"][:, j, :], sb[wt][:, wj, nh * 512:(nh + 1) * 512],
                        start=(ci == 0 and j == 0),
                        stop=(ci == 2 and j == cw // 128 - 1)),
                        reads=[bf(c, "HQT"), bf(c, wt)],
                        writes=[c.ps_buf["PSS%d" % (2 + nh)]], tag="d_mm")
        p.dma(lambda e, t=t: e.dma_start(
            out=sb["SC"][:], in_=c.xs.ap()[t * 128:(t + 1) * 128, :]),
            reads=[c.hb["xs"], bf(c, "SC")], writes=[bf(c, "SC")], tag="xr_ld")
        p.op("dve", lambda e, t=t: e.tensor_tensor(
            sb["SC2"][:], c.ps["PSS"][:, 1024:2048], sb["SC"][:], ALU.add),
            reads=[c.ps_buf["PSS2"], c.ps_buf["PSS3"], bf(c, "SC")],
            writes=[bf(c, "SC2")], tag="d_res")
        if DEBUG_SNAP and t == 0 and seq == 0:
            dbg_snap(c, "p1out_t0", "SC2")
        p.dma(lambda e, t=t: e.dma_start(
            out=c.xs.ap()[t * 128:(t + 1) * 128, :], in_=sb["SC2"][:]),
            reads=[bf(c, "SC2")], writes=[c.hb["xs"]], tag="xn_st")
    return "xs"


def emit_p2(c, seq, x_loc, is_last):
    p, sb = c.p, c.sb
    claim(c, "X", "OW", "QP", "KP", "VP", "WS1", "WS2", "QB1", "KB1", "PT")
    if x_loc != "X":
        emit_load_x(c, seq, x_loc)
    emit_ln_to_xt(c)
    p.dma(lambda e: e.dma_start(
        out=sb["OW"][:], in_=c.ws["Wob"].ap().rearrange("(a q) r -> q a r", q=128)),
        reads=[c.hb["Wob"], bf(c, "OW")], writes=[bf(c, "OW")], tag="wo_ld")
    p.dma(lambda e: e.dma_start(out=sb["RM"][0:64, :], in_=c.ws["RMb"].ap()),
          reads=[c.hb["RMb"], bf(c, "RM")], writes=[bf(c, "RM")], tag="rm_ld")
    p.dma(lambda e: e.dma_start(out=sb["RM"][64:128, :], in_=c.ws["RMb"].ap()),
          reads=[c.hb["RMb"], bf(c, "RM")], writes=[bf(c, "RM")], tag="rm_ld2")
    p.dma(lambda e: e.dma_start(out=sb["WB"][0:1, :],
                                in_=c.w_in["L1_attn_bo"].ap().rearrange("d -> () d")),
          reads=[bf(c, "WB")], writes=[bf(c, "WB")], tag="bo_ld")
    p.op("dve", lambda e: e.tensor_copy(sb["ROWB"][0:1, :], sb["WB"][0:1, :]),
         reads=[bf(c, "WB")], writes=[bf(c, "ROWB")], tag="bo_c", drain=True)
    for pair in range(NPAIR):
        psl = slice(pair * 128, (pair + 1) * 128)
        for j, bn in enumerate(("L1_attn_bq", "L1_attn_bk", "L1_attn_bv")):
            p.dma(lambda e, bn=bn, j=j, psl=psl: e.dma_start(
                out=sb["BCOL"][:, j:j + 1],
                in_=c.w_in[bn].ap().rearrange("(a p) -> p a", p=128)[:, pair:pair + 1]
                if False else
                c.w_in[bn].ap()[psl].rearrange("p -> p ()")),
                reads=[bf(c, "BCOL")], writes=[bf(c, "BCOL")], tag="bc_ld")
        if P2_TRUNC == 10:
            continue
        _load_wslice(c, "Wqb", pair, "WS1")
        _proj_fm(c, "WS1", "QP", bf(c, "QP"), "act",
                 bias_col=None if P2_TRUNC == 11 else sb["BCOL"][:, 0:1],
                 extra_evac_reads=[bf(c, "BCOL")])
        if P2_TRUNC == 12:
            continue
        _load_wslice(c, "Wkb", pair, "WS2")
        _proj_fm(c, "WS2", "KP", bf(c, "KP"), "act",
                 bias_col=None if P2_TRUNC == 11 else sb["BCOL"][:, 1:2],
                 extra_evac_reads=[bf(c, "BCOL")])
        if P2_TRUNC == 13:
            continue
        _load_wslice(c, "Wvb", pair, "WS1")
        _proj_v_tokmajor(c, "WS1", pair, bias_col=None)
        if P2_TRUNC == 1:
            continue
        for half in range(2):
            emit_head_performer(c, pair, half)
    if P2_TRUNC in (1, 2, 3, 10, 11, 12, 13, 14, 15):
        return "X"
    p.dma(lambda e: e.dma_start(
        out=sb["BV8"][:], in_=c.w_in["L1_attn_bv"].ap().rearrange(
            "(a p) -> p a", p=128), allow_slow_non_contiguous=True),
        reads=[bf(c, "BV8")], writes=[bf(c, "BV8")], tag="bv_ld")
    p.op("dve", lambda e: e.tensor_copy(sb["BV8B"][:], sb["BV8"][:]),
         reads=[bf(c, "BV8")], writes=[bf(c, "BV8B")], tag="bv_c", drain=True)
    for nh in range(2):
        for kc in range(NC):
            p.op("pe", lambda e, nh=nh, kc=kc: e.matmul(
                c.ps["PO"][0:1, nh * 512:(nh + 1) * 512],
                sb["BV8B"][:, kc:kc + 1], sb["OW"][:, kc, nh * 512:(nh + 1) * 512],
                start=(kc == 0), stop=(kc == NC - 1)),
                reads=[bf(c, "BV8B"), bf(c, "OW")],
                writes=pob(c, nh * 512, (nh + 1) * 512), tag="bv_mm")
    p.op("dve", lambda e: e.tensor_tensor(sb["ROWB"][0:1, :], c.ps["PO"][0:1, 0:1024],
                                          sb["WB"][0:1, :], ALU.add),
         reads=pob(c, 0, 1024) + [bf(c, "WB"), bf(c, "ROWB")],
         writes=[bf(c, "ROWB")], tag="bv_add", drain=True)
    emit_out_proj(c, target="X", bias_row="ROWB")
    return "X"


def emit_head_performer(c, pair, half):
    p, sb = c.p, c.sb
    h = 2 * pair + half
    hsl = slice(64 * half, 64 * half + 64)
    for mf in range(2):
        for scu in range(4):
            ssl = slice(scu * 512, (scu + 1) * 512)
            quarter = scu % 2
            p.op("pe", lambda e, mf=mf, ssl=ssl, quarter=quarter: e.matmul(
                c.ps["PSS"][:, quarter * 512:(quarter + 1) * 512],
                sb["RM"][hsl, mf * 128:(mf + 1) * 128], sb["QP"][hsl, ssl],
                start=True, stop=True),
                reads=[bf(c, "RM"), bf(c, "QP")],
                writes=[c.ps_buf["PSS%d" % quarter]], tag="pf_qmm")
            emit_elu_plus1(c, quarter,
                           sb["QB1"][:, mf, ssl], bf(c, "QB1"))
    for g in range(4):
        for j in range(4):
            t = g * 4 + j
            p.op("pe", lambda e, t=t, j=j: e.matmul(
                c.ps["PSS"][:, 1024 + j * 256:1024 + (j + 1) * 256],
                sb["KP"][hsl, t * 128:(t + 1) * 128], sb["RM"][hsl, :],
                start=True, stop=True),
                reads=[bf(c, "RM"), bf(c, "KP")],
                writes=[c.ps_buf["PSS%d" % (2 + j // 2)]], tag="pf_kmm")
        emit_elu_plus1_wide(c, (2, 3),
                            sb["KB1"][:, g * 4:(g + 1) * 4, :]
                            .rearrange("p t m -> p (t m)"), bf(c, "KB1"))
    for mf in range(2):
        for t in range(NT):
            p.op("pe", lambda e, mf=mf, t=t, half=half: e.matmul(
                c.ps["PO"][:, mf * 65:mf * 65 + 65],
                sb["KB1"][:, t, mf * 128:(mf + 1) * 128], sb["VP"][:, half, t, :],
                start=(t == 0), stop=(t == NT - 1)),
                reads=[bf(c, "KB1"), bf(c, "VP")], writes=pob(c, 0, 130),
                tag="pf_kv")
    p.op("dve", lambda e: e.tensor_copy(
        sb["KVS"][:].rearrange("p a b -> p (a b)"), c.ps["PO"][:, 0:130]),
        reads=pob(c, 0, 130), writes=[bf(c, "KVS")], tag="pf_kve", drain=True)
    if P2_TRUNC == 2:
        return
    for t in range(NT):
        for mf in range(2):
            p.op("pe", lambda e, t=t, mf=mf: e.matmul(
                c.ps["PO"][:, po_off(t):po_off(t) + 65],
                sb["QB1"][:, mf, t * 128:(t + 1) * 128], sb["KVS"][:, mf, :],
                start=(mf == 0), stop=(mf == 1)),
                reads=[bf(c, "QB1"), bf(c, "KVS")],
                writes=pob(c, po_off(t), po_off(t) + 65), tag="pf_out")
    emit_attn_norm_evac(c, h, clamp=2.56e-4)


def emit_elu_plus1(c, quarter, dst_ap, dst_buf):
    p, sb = c.p, c.sb
    qsl = slice(quarter * 512, (quarter + 1) * 512)
    pb = c.ps_buf["PSS%d" % quarter]
    p.op("dve", lambda e: e.tensor_scalar_min(sb["SC"][:, 0:512],
                                              c.ps["PSS"][:, qsl], 0.0),
         reads=[pb], writes=[bf(c, "SC")], tag="el_min")
    p.op("act", lambda e: e.activation(sb["SC"][:, 0:512], sb["SC"][:, 0:512],
                                       AF.Exp),
         reads=[bf(c, "SC")], writes=[bf(c, "SC")], tag="el_exp")
    p.op("dve", lambda e: e.scalar_tensor_tensor(dst_ap, c.ps["PSS"][:, qsl], 0.0,
                                                 sb["SC"][:, 0:512], ALU.max,
                                                 ALU.add),
         reads=[pb, bf(c, "SC")], writes=[dst_buf], tag="el_stt")


def emit_elu_plus1_wide(c, quarters, dst_ap, dst_buf):
    p, sb = c.p, c.sb
    pbs = [c.ps_buf["PSS%d" % q] for q in quarters]
    p.op("dve", lambda e: e.tensor_scalar_min(sb["SC"][:, 0:1024],
                                              c.ps["PSS"][:, 1024:2048], 0.0),
         reads=pbs, writes=[bf(c, "SC")], tag="el_min")
    p.op("act", lambda e: e.activation(sb["SC"][:, 0:1024], sb["SC"][:, 0:1024],
                                       AF.Exp),
         reads=[bf(c, "SC")], writes=[bf(c, "SC")], tag="el_exp")
    p.op("dve", lambda e: e.scalar_tensor_tensor(dst_ap, c.ps["PSS"][:, 1024:2048],
                                                 0.0, sb["SC"][:, 0:1024], ALU.max,
                                                 ALU.add),
         reads=pbs + [bf(c, "SC")], writes=[dst_buf], tag="el_stt")


def emit_p3(c, seq, x_loc, is_last):
    p, sb = c.p, c.sb
    claim(c, "X")
    if x_loc != "X":
        emit_load_x(c, seq, x_loc)
    emit_ln_stats(c, lambda t: sb["X"][:, t, :], lambda t: [bf(c, "X")])
    for t in range(NT):
        emit_ln_apply(c, t, sb["X"][:, t, :], sb["SC"][:],
                      reads=[bf(c, "X")], writes=[bf(c, "SC")])
        emit_act_quant(c, "SC", sb["SC"][:], 1024, sb["QOUT"][:], bf(c, "QOUT"),
                       sm_col=t)
        if DEBUG_SNAP and t == 0 and seq == 0:
            dbg_snap(c, "p3xq_t0", "QOUT")
            dbg_snap(c, "p3sc_t0", "SC")
            dbg_snap(c, "p3sm_t0", "SM", ap=sb["SM"][:, 32:35])
        emit_xt_from_bf16_tiles(c, t, sb["QOUT"][:], bf(c, "QOUT"))
    p.dma(lambda e: e.dma_start(
        out=c.xs.ap().rearrange("(t p) d -> p t d", p=128), in_=sb["X"][:]),
        reads=[bf(c, "X")], writes=[c.hb["xs"]], tag="x_park3")
    for hf in range(2):
        claim(c, "H1", "WG1c", "WU1c")
        p.op("dve", lambda e: e.memset(sb["SM2"][:, 0:8], 0.0),
             writes=[bf(c, "SM2")], tag="p3_z")
        for ci in range(4):
            for wname, wsrc in (("WG1c", "wgT1"), ("WU1c", "wuT1")):
                p.dma(lambda e, wname=wname, wsrc=wsrc, ci=ci: e.dma_start(
                    out=sb[wname][:],
                    in_=c.ws[wsrc].ap().rearrange("(a q) r -> q a r", q=128)
                        [:, :, ci * 1024:(ci + 1) * 1024]),
                    reads=[c.hb[wsrc], bf(c, wname)], writes=[bf(c, wname)],
                    tag="p3_wld")
            for tj in range(8):
                t = hf * 8 + tj
                for half in range(2):
                    qsl = slice(half * 512, half * 512 + 512)
                    for kc in range(NC):
                        p.op("pe", lambda e, t=t, kc=kc, qsl=qsl: e.matmul(
                            c.ps["PSS"][:, qsl],
                            sb["XT"][:, kc, t * 128:(t + 1) * 128],
                            sb["WG1c"][:, kc, qsl], start=(kc == 0),
                            stop=(kc == NC - 1)),
                            reads=[bf(c, "XT"), bf(c, "WG1c")],
                            writes=[c.ps_buf["PSS%d" % half]], tag="p3_gmm")
                    for kc in range(NC):
                        p.op("pe", lambda e, t=t, kc=kc, qsl=qsl: e.matmul(
                            c.ps["PO"][:, qsl],
                            sb["XT"][:, kc, t * 128:(t + 1) * 128],
                            sb["WU1c"][:, kc, qsl], start=(kc == 0),
                            stop=(kc == NC - 1)),
                            reads=[bf(c, "XT"), bf(c, "WU1c")],
                            writes=pob(c, qsl.start, qsl.stop), tag="p3_umm")
                p.op("dve", lambda e, t=t: e.tensor_tensor(
                    sb["SM2"][:, 8:9], sb["SM"][:, t:t + 1], sb["CN"][:, 1:2],
                    ALU.mult),
                    reads=[bf(c, "SM"), bf(c, "CN")], writes=[bf(c, "SM2")],
                    tag="p3_dqg")
                p.op("dve", lambda e, t=t: e.tensor_tensor(
                    sb["SM2"][:, 9:10], sb["SM"][:, t:t + 1], sb["CN"][:, 2:3],
                    ALU.mult),
                    reads=[bf(c, "SM"), bf(c, "CN")], writes=[bf(c, "SM2")],
                    tag="p3_dqu", drain=True)
                p.op("act", lambda e: e.activation(sb["SC"][:, 0:1024],
                                                   c.ps["PSS"][:, 0:1024], AF.Silu,
                                                   scale=sb["SM2"][:, 8:9]),
                     reads=[c.ps_buf["PSS0"], c.ps_buf["PSS1"], bf(c, "SM2")],
                     writes=[bf(c, "SC")], tag="p3_silu")
                p.op("dve", lambda e, tj=tj, ci=ci: e.scalar_tensor_tensor(
                    sb["H1"][:, tj, ci * 1024:(ci + 1) * 1024],
                    c.ps["PO"][:, 0:1024], sb["SM2"][:, 9:10], sb["SC"][:, 0:1024],
                    ALU.mult, ALU.mult),
                    reads=pob(c, 0, 1024) + [bf(c, "SM2"), bf(c, "SC")],
                    writes=[bf(c, "H1")], tag="p3_h")
                if DEBUG_SNAP and tj == 0 and ci == 0 and hf == 0 and seq == 0:
                    dbg_snap(c, "p3h_t0", "H1", ap=sb["H1"][:, 0, 0:1024])
                p.op("dve", lambda e, tj=tj, ci=ci: e.tensor_reduce(
                    sb["SM2"][:, 16 + tj:17 + tj],
                    sb["H1"][:, tj, ci * 1024:(ci + 1) * 1024], AX.X, ALU.max,
                    apply_absolute_value=True),
                    reads=[bf(c, "H1")], writes=[bf(c, "SM2")], tag="p3_hm")
                p.op("dve", lambda e, tj=tj: e.tensor_tensor(
                    sb["SM2"][:, tj:tj + 1], sb["SM2"][:, tj:tj + 1],
                    sb["SM2"][:, 16 + tj:17 + tj], ALU.max),
                    reads=[bf(c, "SM2")], writes=[bf(c, "SM2")], tag="p3_hmu")
        claim(c, "WD1a", "WD1b")
        p.dma(lambda e: e.dma_start(
            out=sb["WD1a"][:],
            in_=c.ws["wdT1"].ap().rearrange("(a q) r -> q a r", q=128)[:, 0:16, :]),
            reads=[c.hb["wdT1"], bf(c, "WD1a")], writes=[bf(c, "WD1a")],
            tag="p3_wda")
        p.dma(lambda e: e.dma_start(
            out=sb["WD1b"][:],
            in_=c.ws["wdT1"].ap().rearrange("(a q) r -> q a r", q=128)[:, 16:32, :]),
            reads=[c.hb["wdT1"], bf(c, "WD1b")], writes=[bf(c, "WD1b")],
            tag="p3_wdb")
        for tj in range(8):
            t = hf * 8 + tj
            p.op("dve", lambda e, tj=tj: e.tensor_scalar_max(
                sb["SM2"][:, 24:25], sb["SM2"][:, tj:tj + 1], 1e-5),
                reads=[bf(c, "SM2")], writes=[bf(c, "SM2")], tag="p3_sc")
            p.op("dve", lambda e: e.reciprocal(sb["SM2"][:, 25:26],
                                               sb["SM2"][:, 24:25]),
                 reads=[bf(c, "SM2")], writes=[bf(c, "SM2")], tag="p3_sr")
            p.op("dve", lambda e: e.tensor_scalar_mul(sb["SM2"][:, 26:27],
                                                      sb["SM2"][:, 25:26], 127.0),
                 reads=[bf(c, "SM2")], writes=[bf(c, "SM2")], tag="p3_s")
            p.op("dve", lambda e: e.tensor_scalar_mul(sb["SM2"][:, 27:28],
                                                      sb["SM2"][:, 24:25],
                                                      1.0 / 127.0),
                 reads=[bf(c, "SM2")], writes=[bf(c, "SM2")], tag="p3_i")
            p.op("dve", lambda e: e.tensor_tensor(sb["SM2"][:, 27:28],
                                                  sb["SM2"][:, 27:28],
                                                  sb["CN"][:, 3:4], ALU.mult),
                 reads=[bf(c, "SM2"), bf(c, "CN")], writes=[bf(c, "SM2")],
                 tag="p3_dq2")
            for ci in range(4):
                p.op("dve", lambda e, tj=tj, ci=ci: e.tensor_single_scalar(
                    sb["SC2"][:, 0:1024], sb["H1"][:, tj, ci * 1024:(ci + 1) * 1024],
                    sb["SM2"][:, 26:27], ALU.mult),
                    reads=[bf(c, "H1"), bf(c, "SM2")], writes=[bf(c, "SC2")],
                    tag="p3_qm0")
                p.op("dve", lambda e: e.tensor_scalar(
                    sb["SC2"][:, 0:1024], sb["SC2"][:, 0:1024],
                    -128.0, 127.0, ALU.max, ALU.min),
                    reads=[bf(c, "SC2")], writes=[bf(c, "SC2")], tag="p3_qcf")
                p.op("dve", lambda e: e.tensor_scalar_add(
                    sb["SC2"][:, 0:1024], sb["SC2"][:, 0:1024], MAGIC),
                    reads=[bf(c, "SC2")], writes=[bf(c, "SC2")], tag="p3_qm")
                p.op("dve", lambda e: e.tensor_single_scalar(
                    sb["QI"][:, 0:1024], sb["SC2"][:, 0:1024].bitcast(I32),
                    0, ALU.bitwise_or),
                    reads=[bf(c, "SC2")], writes=[bf(c, "QI")], tag="p3_qc")
                p.op("dve", lambda e: e.tensor_scalar_add(
                    sb["QOUT"][:, 0:1024], sb["QI"][:, 0:1024].bitcast(F32), -MAGIC),
                    reads=[bf(c, "QI")], writes=[bf(c, "QOUT")], tag="p3_qb")
                _transpose_1024(c, sb["QOUT"][:], bf(c, "QOUT"), 8)
                p.op("dve", lambda e: e.tensor_copy(
                    sb["HQT"][:].rearrange("p a b -> p (a b)"), c.ps["PX"][:]),
                    reads=[c.ps_buf["PX"]], writes=[bf(c, "HQT")], tag="p3_he")
                for j in range(8):
                    kcg = ci * 8 + j
                    wt = "WD1a" if kcg < 16 else "WD1b"
                    wj = kcg if kcg < 16 else kcg - 16
                    for nh in range(2):
                        p.op("pe", lambda e, j=j, wt=wt, wj=wj, nh=nh, ci=ci: e.matmul(
                            c.ps["PSS"][:, 1024 + nh * 512:1024 + nh * 512 + 512],
                            sb["HQT"][:, j, :],
                            sb[wt][:, wj, nh * 512:(nh + 1) * 512],
                            start=(ci == 0 and j == 0), stop=(ci == 3 and j == 7)),
                            reads=[bf(c, "HQT"), bf(c, wt)],
                            writes=[c.ps_buf["PSS%d" % (2 + nh)]], tag="p3_dmm")
            p.dma(lambda e, t=t: e.dma_start(
                out=sb["SC"][:], in_=c.xs.ap()[t * 128:(t + 1) * 128, :]),
                reads=[c.hb["xs"], bf(c, "SC")], writes=[bf(c, "SC")], tag="p3_xl")
            p.op("dve", lambda e, t=t: e.scalar_tensor_tensor(
                sb["SC2"][:], c.ps["PSS"][:, 1024:2048], sb["SM2"][:, 27:28],
                sb["SC"][:], ALU.mult, ALU.add),
                reads=[c.ps_buf["PSS2"], c.ps_buf["PSS3"], bf(c, "SM2"),
                       bf(c, "SC")],
                writes=[bf(c, "SC2")], tag="p3_res")
            if DEBUG_SNAP and t == 0 and seq == 0:
                dbg_snap(c, "p3out_t0", "SC2")
            if is_last:
                p.dma(lambda e, t=t, seq=seq: e.dma_start(
                    out=c.out.ap()[seq, t * 128:(t + 1) * 128, :], in_=sb["SC2"][:]),
                    reads=[bf(c, "SC2")], writes=[c.hb["out"]], tag="p3_out")
            else:
                p.dma(lambda e, t=t: e.dma_start(
                    out=c.xs.ap()[t * 128:(t + 1) * 128, :], in_=sb["SC2"][:]),
                    reads=[bf(c, "SC2")], writes=[c.hb["xs"]], tag="p3_xs")
    return "out" if is_last else "xs"


def emit_act_quant(c, src_name, src_ap, n, dst_ap, dst_buf, sm_col):
    p, sb = c.p, c.sb
    srcb = bf(c, src_name)
    p.op("dve", lambda e: e.tensor_reduce(sb["SM"][:, 32:33], src_ap, AX.X, ALU.max,
                                          apply_absolute_value=True),
         reads=[srcb], writes=[bf(c, "SM")], tag="aq_max")
    p.op("dve", lambda e: e.tensor_scalar_max(sb["SM"][:, 32:33], sb["SM"][:, 32:33],
                                              1e-5),
         reads=[bf(c, "SM")], writes=[bf(c, "SM")], tag="aq_cl")
    p.op("dve", lambda e, sm_col=sm_col: e.tensor_scalar_mul(
        sb["SM"][:, sm_col:sm_col + 1], sb["SM"][:, 32:33], 1.0 / 127.0),
        reads=[bf(c, "SM")], writes=[bf(c, "SM")], tag="aq_inv")
    p.op("dve", lambda e: e.reciprocal(sb["SM"][:, 33:34], sb["SM"][:, 32:33]),
         reads=[bf(c, "SM")], writes=[bf(c, "SM")], tag="aq_rcp")
    p.op("dve", lambda e: e.tensor_scalar_mul(sb["SM"][:, 34:35], sb["SM"][:, 33:34],
                                              127.0),
         reads=[bf(c, "SM")], writes=[bf(c, "SM")], tag="aq_s")
    p.op("dve", lambda e, n=n: e.tensor_single_scalar(sb["SC2"][:, 0:n], src_ap,
                                                      sb["SM"][:, 34:35], ALU.mult),
         reads=[srcb, bf(c, "SM")], writes=[bf(c, "SC2")], tag="aq_m0")
    p.op("dve", lambda e, n=n: e.tensor_scalar(sb["SC2"][:, 0:n],
                                               sb["SC2"][:, 0:n],
                                               -128.0, 127.0, ALU.max, ALU.min),
         reads=[bf(c, "SC2")], writes=[bf(c, "SC2")], tag="aq_clf")
    p.op("dve", lambda e, n=n: e.tensor_scalar_add(sb["SC2"][:, 0:n],
                                                   sb["SC2"][:, 0:n], MAGIC),
         reads=[bf(c, "SC2")], writes=[bf(c, "SC2")], tag="aq_m")
    if DEBUG_SNAP and sm_col == 0:
        dbg_snap(c, "aqm_sc2", "SC2")
    p.op("dve", lambda e, n=n: e.tensor_single_scalar(sb["QI"][:, 0:n],
                                                      sb["SC2"][:, 0:n].bitcast(I32),
                                                      0, ALU.bitwise_or),
         reads=[bf(c, "SC2")], writes=[bf(c, "QI")], tag="aq_c")
    p.op("dve", lambda e, n=n: e.tensor_scalar_add(dst_ap,
                                                   sb["QI"][:, 0:n].bitcast(F32),
                                                   -MAGIC),
         reads=[bf(c, "QI")], writes=[dst_buf], tag="aq_b")


def emit_epilogue(c, seq, x_loc):
    p, sb = c.p, c.sb
    toks = []
    if x_loc == "out":
        pass
    elif x_loc == "X":
        toks.append(p.dma(lambda e, seq=seq: e.dma_start(
            out=c.out.ap()[seq].rearrange("(t p) d -> p t d", p=128), in_=sb["X"][:]),
            reads=[bf(c, "X")], writes=[c.hb["out"]], tag="ep_x"))
    elif x_loc == "xs":
        toks.append(p.dma(lambda e, seq=seq: e.dma_start(
            out=c.out.ap()[seq], in_=c.xs.ap()),
            reads=[c.hb["xs"]], writes=[c.hb["out"]], tag="ep_xs"))
    else:
        toks.append(p.dma(lambda e, seq=seq: e.dma_start(
            out=c.out.ap()[seq], in_=c.x_in.ap()[seq]),
            reads=[c.hb["x"]], writes=[c.hb["out"]], tag="ep_in"))
    if x_loc == "out":
        for rec in c.p.stream["qs"]:
            if rec["tag"] == "p3_out":
                toks.append(rec["tok"])
    return toks


PHASE_FNS = {0: emit_p0, 1: emit_p1, 2: emit_p2, 3: emit_p3}

_NC_CACHE = {}


def get_program(phases=(0, 1, 2, 3)):
    key = tuple(phases)
    if key not in _NC_CACHE:
        _NC_CACHE[key] = build_program(phases)
    return _NC_CACHE[key]


def kernel(**inputs):
    x = np.ascontiguousarray(inputs["L0_attn_x"], dtype=np.float32)
    nc = get_program()
    in_maps = []
    for core in range(NCORES):
        m = {"x": x[SEQ_PER_CORE * core:SEQ_PER_CORE * (core + 1)]}
        for n in WEIGHT_NAMES:
            m[n] = np.ascontiguousarray(inputs[n], dtype=np.float32)
        in_maps.append(m)
    res = run_bass_kernel_spmd(nc, in_maps, core_ids=list(range(NCORES)))
    out = np.concatenate([np.asarray(r["out"], dtype=np.float32)
                          for r in res.results], axis=0)
    return out.reshape(B, S, D)
